# version 02
# baseline (speedup 1.0000x reference)
import contextlib

import numpy as np
import ml_dtypes

import concourse.bass as bass
import concourse.mybir as mybir
from concourse.bass_utils import run_bass_kernel_spmd

F32 = mybir.dt.float32
BF16 = mybir.dt.bfloat16
AF = mybir.ActivationFunctionType
ALU = mybir.AluOpType
AX = mybir.AxisListType
BF = ml_dtypes.bfloat16

T = 2048
NCH = 16
D = 1024
DC = 8
H = 8
HD = 128
FF = 4096
FC = 32
NL = 2
EPS = 1e-5
S_FULL = 8192
N_CORES = 8

STAGE = "final"
_ORDER = ["echo", "stats", "m0", "dbgret", "yraw0", "wo0", "ffn0",
          "yraw1", "wo1", "ffn1", "x2", "final"]


def _gammas():
    return 1.0 - np.exp(np.linspace(np.log(1.0 / 32), np.log(1.0 / 512), H))


def _xpos_tables():
    scale_vec = (np.arange(0, HD, 2) + 0.4 * HD) / (1.4 * HD)
    power = (np.arange(S_FULL, dtype=np.float64) / 512)[:, None]
    scale = scale_vec[None, :] ** power
    half = HD // 2
    inv_freq = 1.0 / 10000.0 ** (np.arange(half, dtype=np.float64) / half)
    ang = np.arange(S_FULL, dtype=np.float64)[:, None] * inv_freq[None, :]
    sin, cos = np.sin(ang), np.cos(ang)
    dup = lambda m: np.repeat(m, 2, axis=-1)
    return (dup(sin * scale), dup(cos * scale), dup(sin / scale), dup(cos / scale))


class Emit:
    ENGS = ("t", "v", "s", "g", "y")

    def __init__(self):
        self.q = {e: [] for e in self.ENGS}
        self.c = {}

    def sem(self, name):
        self.c[name] = 0

    def op(self, eng, fn, waits=(), inc=None, n=1):
        for s, v in waits:
            assert v <= self.c[s], f"wait {s}>={v} but only {self.c[s]} emitted"
        if inc is not None:
            assert fn is not None
            self.c[inc] += n
        self.q[eng].append((tuple(waits), fn, inc, n))
        return self.c.get(inc)

    def qsems(self, queue, depth):
        names = [f"{queue}{j}" for j in range(depth)]
        for n in names:
            self.sem(n)
        self.qpool = getattr(self, "qpool", {})
        self.qpool[queue] = {"names": names, "idx": 0}
        return names

    def dma(self, eng, queue, fn, waits=()):
        pool = self.qpool[queue]
        sem = pool["names"][pool["idx"]]
        pool["idx"] = (pool["idx"] + 1) % len(pool["names"])
        w = [(sem, self.c[sem])] + list(waits)
        self.op(eng, fn, w, sem, 16)
        return (sem, self.c[sem])

    def qfull(self, queue):
        return [(n, self.c[n]) for n in self.qpool[queue]["names"]]

    def t(self, fn, waits=()):
        return self.op("t", fn, waits, "t", 1)

    def v(self, fn, waits=()):
        return self.op("v", fn, [("v", self.c["v"])] + list(waits), "v", 1)

    def s(self, fn, waits=()):
        return self.op("s", fn, [("s", self.c["s"])] + list(waits), "s", 1)


def build_program(stage):
    nc = bass.Bass("TRN2", target_bir_lowering=False, debug=False)
    E = Emit()
    stage_i = _ORDER.index(stage)

    gam = _gammas()
    g128 = [float(g) ** 128 for g in gam]

    def din(name, shape, dt=F32):
        return nc.dram_tensor(name, shape, dt, kind="ExternalInput").ap()

    x0_d = din("x0", [NCH, 128, D])
    y_d = nc.dram_tensor("y", [NCH, 128, D], F32, kind="ExternalOutput").ap()

    wq_d, wk_d, wv_d, wg_d, wo_d, w1_d, w2_d = [], [], [], [], [], [], []
    lnw_d = {}
    for i in range(NL):
        wq_d.append(din(f"wq{i}", [H, 128, DC, HD]))
        wk_d.append(din(f"wk{i}", [H, 128, DC, HD]))
        wv_d.append(din(f"wv{i}", [H, 128, DC, HD]))
        wg_d.append(din(f"wg{i}", [128, DC, D]))
        wo_d.append(din(f"wo{i}", [128, DC, D]))
        w1_d.append(din(f"w1_{i}", [FC, 128, DC, 128]))
        w2_d.append(din(f"w2_{i}", [4, 128, 8, D]))
        for nm in ("ln1w", "ln1b", "ln2w", "ln2b", "gnw", "gnb", "b2bc"):
            lnw_d[(i, nm)] = din(f"{nm}{i}", [128, D])
        lnw_d[(i, "b1pp")] = din(f"b1pp{i}", [128, FC])
    wi_d = din("wiT", [FC, 128, DC, 128])
    wom_d = din("woT", [4, 128, 8, D])
    gw_d = din("gwT", [128, DC, 8])
    acat_d = din("acatT", [128, DC, 32])
    bcat_d = din("bcat", [32, FF])
    gateb_d = din("gatebbc", [128, 8])
    wib_d = din("wibpp", [128, FC])
    wob_d = din("wobbc", [128, D])
    tabs_d = din("tabs", [4, 128, NCH, 128])
    caus_d = din("caus", [128, 128])
    gq_d = din("gq", [128, H, 128])
    gk_d = din("gk", [128, H, 128])
    id_d = din("identbf", [128, 128], BF16)
    idf_d = din("identf32", [128, 128])
    rm_d = din("rmat", [128, 128], BF16)
    mw_d = din("mw", [128, H, 4])
    cpool_d = din("cpool", [128, 12])

    dbg_d = {n: nc.dram_tensor(n, [128, T], F32, kind="ExternalOutput").ap()
             for n in ("dbgq", "dbgk", "dbgv", "dbgy", "dbgs")} if stage == "dbgret" else {}
    if stage == "dbgret":
        dbg_d["dbga"] = nc.dram_tensor("dbga", [128, 8192], F32, kind="ExternalOutput").ap()
        dbg_d["dbgb"] = nc.dram_tensor("dbgb", [128, 8192], F32, kind="ExternalOutput").ap()
    x2_d = nc.dram_tensor("x2spill", [NCH, 128, D], F32).ap()
    qsp_d = nc.dram_tensor("qspill", [NL, H, 128, T], BF16).ap()
    cc_in = [nc.dram_tensor(f"ccin{i}", [H, 128, HD], F32) for i in range(NL)]
    cc_out = [nc.dram_tensor(f"ccout{i}", [4 * H, 128, HD], F32) for i in range(NL)]

    sem_names = ["t", "v", "s", "g", "cc"]
    for sname in sem_names:
        E.sem(sname)
    sem_names += E.qsems("qY", 4)
    sem_names += E.qsems("qW", 6)
    sem_names += E.qsems("qS", 4)
    sem_names += E.qsems("qO", 4)

    es = contextlib.ExitStack()
    with es:
        sb = lambda name, shape, dt: es.enter_context(nc.sbuf_tensor(name, shape, dt))
        xres = sb("xres", [128, NCH, D], F32)
        a1 = sb("a1", [128, DC * T], BF16)
        a2 = sb("a2", [128, NCH * D], BF16)
        wqkv = [sb(f"wqkv{j}", [128, 3 * DC, HD], BF16) for j in range(2)]
        hbuf = sb("hbuf", [128, 10240], BF16)
        lnw = sb("lnw", [128, 2, D], F32)
        idbf = sb("idbf", [128, 128], BF16)
        idf32 = sb("idf32", [128, 128], F32)
        rmat = sb("rmats", [128, 128], BF16)
        mw = sb("mws", [128, H, 4], F32)
        cpool = sb("cpools", [128, 12], F32)
        sigma = sb("sigma", [128, HD], F32)
        sbf = sb("sbf", [128, HD], BF16)
        sinb = sb("sinb", [128, HD], BF16)
        mg = sb("mg", [128, 4, HD], F32)
        stat = sb("stat", [128, 40], F32)
        gnst = sb("gnst", [128, 6, NCH * H], F32)
        bpp = sb("bpp", [128, 2, FC], F32)
        pA = es.enter_context(nc.psum_tensor("pA", [128, 512], F32))
        pB = es.enter_context(nc.psum_tensor("pB", [128, 512], F32))
        pC = es.enter_context(nc.psum_tensor("pC", [128, 1024], F32))
        pD = es.enter_context(nc.psum_tensor("pD", [128, 1024], F32))
        pb = es.enter_context(nc.psum_tensor("pb", [128, 1024], BF16))
        pb2 = es.enter_context(nc.psum_tensor("pb2", [128, 1024], BF16))

        sems = {s: es.enter_context(nc.semaphore(s)) for s in sem_names}

        qp = hbuf[:, 0:2048]
        kp = hbuf[:, 2048:4096]
        kd3 = hbuf[:, 4096:6144].rearrange("p (c k) -> p c k", k=128)
        vt3 = hbuf[:, 6144:8192].rearrange("p (c k) -> p c k", k=128)
        tst = hbuf[:, 8192:10240].rearrange("p (a b) -> p a b", b=512)
        wgq = hbuf[:, 2048:6144].rearrange("p (d c) -> p d c", d=DC)
        woh = hbuf[:, 6144:10240].rearrange("p (d c) -> p d c", d=DC)
        gtmp = hbuf[:, 0:2048].rearrange("p (a b) -> p a b", b=512)
        w2g = hbuf[:, 0:8192].rearrange("p (f d) -> p f d", f=8)
        wst = hbuf[:, 8192:10240].rearrange("p (a d k) -> p a d k", a=2, d=DC)

        st_mean = stat[:, 0:16]
        st_rstd = stat[:, 16:32]
        st_t1 = stat[:, 32:33]
        st_mx = stat[:, 33:34]

        a1t = lambda dc, t0, t1: a1[:, dc * T + t0: dc * T + t1]
        a2c = lambda c: a2[:, c * D:(c + 1) * D]
        a2ch = lambda c, h: a2[:, c * D + h * HD: c * D + (h + 1) * HD]
        h1sl = lambda fcg, t0, t1: a2[:, fcg * T + t0: fcg * T + t1]

        def dmay(fn, waits=()):
            return E.dma("y", "qY", fn, waits)

        def dmag(fn, waits=()):
            return E.dma("g", "qW", fn, waits)

        def dmas(fn, waits=()):
            return E.dma("s", "qS", fn, waits)

        def dmao(fn, waits=()):
            return E.dma("s", "qO", fn, waits)

        ld_x0 = dmay(lambda y: y.dma_start(
            out=xres[:], in_=x0_d.rearrange("c p d -> p c d")))

        es2 = contextlib.ExitStack()
        es.callback(es2.close)
        tabs = es2.enter_context(nc.sbuf_tensor("tabss", [128, 4, NCH, 128], BF16))
        caus = es2.enter_context(nc.sbuf_tensor("causs", [128, 128], F32))
        gq = es2.enter_context(nc.sbuf_tensor("gqs", [128, H, 128], BF16))
        gk = es2.enter_context(nc.sbuf_tensor("gks", [128, H, 128], BF16))
        t1 = dmag(lambda g: g.dma_start(
            out=tabs[:], in_=tabs_d.rearrange("t p c k -> p t c k")))
        t2 = dmag(lambda g: g.dma_start(out=gq[:], in_=gq_d))
        t3 = dmag(lambda g: g.dma_start(out=gk[:], in_=gk_d))
        for ap_, dst in [(caus_d, caus), (id_d, idbf),
                         (idf_d, idf32), (rm_d, rmat), (mw_d, mw), (cpool_d, cpool)]:
            dmay(lambda y, a=ap_, d=dst: y.dma_start(out=d[:], in_=a))
        ld_consts = E.qfull("qY") + [t1, t2, t3]

        def ln_stats(src3, wait_v=(), wait_s=()):
            for c in range(NCH):
                bank = pC if c % 2 == 0 else pD
                E.v(lambda v, c=c: v.reduce_sum(
                    st_mean[:, c:c + 1], src3[:, c, :], axis=AX.X), waits=wait_v)
                E.v(lambda v, c=c, b=bank: v.tensor_mul(
                    b[:], src3[:, c, :], src3[:, c, :]))
                E.v(lambda v, c=c, b=bank: v.reduce_sum(
                    st_rstd[:, c:c + 1], b[:], axis=AX.X))
            v_sum = E.c["v"]
            E.v(lambda v: v.tensor_scalar(
                st_mean[:], st_mean[:], 1.0 / D, None, ALU.mult),
                waits=[("v", v_sum)])
            E.v(lambda v: v.tensor_scalar(
                st_rstd[:], st_rstd[:], 1.0 / D, None, ALU.mult))
            E.v(lambda v: v.tensor_mul(gnst[:, 4, 0:16], st_mean[:], st_mean[:]))
            E.v(lambda v: v.tensor_sub(st_rstd[:], st_rstd[:], gnst[:, 4, 0:16]))
            v_fix = E.c["v"]
            s_std = E.s(lambda s: s.activation(
                st_rstd[:], st_rstd[:], AF.Sqrt, bias=cpool[:, 0:1]),
                waits=[("v", v_fix)] + ld_consts)
            v_r = E.v(lambda v: v.reciprocal(st_rstd[:], st_rstd[:]),
                      waits=[("s", s_std)])
            return v_r

        xres3 = xres

        def transpose_block_to_a1(src_c_ap_fn, wait_v_fn, dst_col_of_c, nblk=DC,
                                  extra_t_waits=()):
            evs = []
            for c in range(NCH):
                bank = pb if c % 2 == 0 else pb2
                wv = wait_v_fn(c)
                tws = list(extra_t_waits) + ([("v", wv)] if wv else [])
                if len(evs) >= 2:
                    tws.append(("v", evs[-2]))
                for j in range(nblk):
                    tt = E.t(lambda t, c=c, j=j, b=bank: t.transpose(
                        b[:, j * 128:(j + 1) * 128], src_c_ap_fn(c, j), idbf[:]),
                        waits=tws if j == 0 else ())
                t_done = E.c["t"]
                ev = E.v(lambda v, c=c, b=bank: v.tensor_copy(
                    a1[:].rearrange("p (d t) -> p d t", d=DC)[:, :, c * 128:(c + 1) * 128],
                    b[:].rearrange("p (d t) -> p d t", d=DC)),
                    waits=[("t", t_done)])
                evs.append(ev)
            return E.c["v"]

        if stage == "echo":
            for c in range(NCH):
                dmao(lambda s, c=c: s.dma_start(
                    out=y_d[c], in_=xres[:, c, :]), waits=[ld_x0])
            E.op("s", None, waits=E.qfull("qO"))

        if stage == "stats":
            for c in range(NCH):
                bank = pC if c % 2 == 0 else pD
                E.v(lambda v, c=c: v.reduce_sum(
                    st_mean[:, c:c + 1], xres3[:, c, :], axis=AX.X),
                    waits=[ld_x0])
                E.v(lambda v, c=c, b=bank: v.tensor_mul(
                    b[:], xres3[:, c, :], xres3[:, c, :]))
                E.v(lambda v, c=c, b=bank: v.reduce_sum(
                    st_rstd[:, c:c + 1], b[:], axis=AX.X))
            dmao(lambda s: s.dma_start(out=y_d[0][:, 0:40], in_=stat[:]),
                 waits=[("v", E.c["v"])])
            E.v(lambda v: v.tensor_scalar(
                st_mean[:], st_mean[:], 1.0 / D, None, ALU.mult),
                waits=E.qfull("qO"))
            E.v(lambda v: v.tensor_scalar(
                st_rstd[:], st_rstd[:], 1.0 / D, None, ALU.mult))
            E.v(lambda v: v.tensor_mul(gnst[:, 4, 0:16], st_mean[:], st_mean[:]))
            E.v(lambda v: v.tensor_sub(st_rstd[:], st_rstd[:], gnst[:, 4, 0:16]))
            dmao(lambda s: s.dma_start(out=y_d[1][:, 0:40], in_=stat[:]),
                 waits=[("v", E.c["v"])])
            s_std = E.s(lambda s: s.activation(
                st_rstd[:], st_rstd[:], AF.Sqrt, bias=cpool[:, 0:1]),
                waits=[("v", E.c["v"])] + ld_consts + E.qfull("qO"))
            E.v(lambda v: v.reciprocal(st_rstd[:], st_rstd[:]),
                waits=[("s", s_std)])
            dmao(lambda s: s.dma_start(out=y_d[2][:, 0:40], in_=stat[:]),
                 waits=[("v", E.c["v"])])
            E.op("s", None, waits=E.qfull("qO"))
        if stage not in ("echo", "stats"):
            vr = ln_stats(xres3, wait_v=[ld_x0], wait_s=[ld_x0])
            for c in range(NCH):
                E.v(lambda v, c=c: v.tensor_scalar(
                    xres3[:, c, :], xres3[:, c, :],
                    st_mean[:, c:c + 1], st_rstd[:, c:c + 1],
                    ALU.subtract, ALU.mult), waits=[("v", vr)] if c == 0 else ())
        v_m0 = E.c["v"]

        def tap_f32(src3, wv):
            for c in range(NCH):
                dmao(lambda s, c=c: s.dma_start(
                    out=y_d[c], in_=src3[:, c, :]),
                    waits=[("v", wv), ("s", E.c["s"])])
            E.op("s", None, waits=E.qfull("qO"))

        def tap_bf16(flat_src, wv, ws):
            for c in range(NCH):
                E.dma("g", "qO", lambda g, c=c: g.dma_start(
                    out=y_d[c], in_=flat_src[:, c * D:(c + 1) * D]),
                    waits=[("v", wv), ("s", ws)])
            E.op("g", None, waits=E.qfull("qO"))

        if stage == "m0":
            tap_f32(xres3, v_m0)

        def ln_to_a1(i, wkey, bkey, v_src_ready, s_src_ready, t_free_a1):
            ld0 = dmay(lambda y: y.dma_start(out=lnw[:, 0, :], in_=lnw_d[(i, wkey)]),
                       waits=[("v", v_src_ready)])
            ld1 = dmay(lambda y: y.dma_start(out=lnw[:, 1, :], in_=lnw_d[(i, bkey)]),
                       waits=[("v", v_src_ready)])
            vr = ln_stats(xres3, wait_v=[("v", v_src_ready)],
                          wait_s=[("s", s_src_ready), ("v", v_src_ready)])
            for c in range(NCH):
                E.v(lambda v, c=c: v.tensor_scalar(
                    a2[:, c * D:(c + 1) * D], xres3[:, c, :],
                    st_mean[:, c:c + 1], st_rstd[:, c:c + 1],
                    ALU.subtract, ALU.mult), waits=[("v", vr)] if c == 0 else ())
            a2v = a2[:].rearrange("p (c d) -> p c d", c=NCH)
            lnw0b = lnw[:, 0:1, :].broadcast_to([128, NCH, D])
            lnw1b = lnw[:, 1:2, :].broadcast_to([128, NCH, D])
            E.v(lambda v: v.tensor_tensor(a2v, a2v, lnw0b, ALU.mult),
                waits=[ld0])
            v_stage = E.v(lambda v: v.tensor_tensor(a2v, a2v, lnw1b, ALU.add),
                          waits=[ld1])
            return transpose_block_to_a1(
                lambda c, j: a2[:, c * D + j * 128: c * D + (j + 1) * 128],
                lambda c: v_stage, None, extra_t_waits=[("t", t_free_a1)])

        def retention(i, v_a1_ready):
            woffs = {"q": (0, wq_d[i]), "k": (DC, wk_d[i]), "v": (2 * DC, wv_d[i])}

            def load_head(h):
                lds = {}
                for kk, (off, wd) in woffs.items():
                    lds[kk] = dmag(lambda g, wd=wd, h=h, off=off: g.dma_start(
                        out=wqkv[h % 2][:, off:off + DC, :], in_=wd[h]),
                        waits=[("t", E.c["t"])])
                return lds

            ldh = [load_head(0), load_head(1)]
            pCa, pCb = pC[:, 0:512], pC[:, 512:1024]
            pDa, pDb = pD[:, 0:128], pD[:, 512:640]
            yCa, yCb = pC[:, 0:128], pC[:, 512:640]

            for h in range(H):
                slot = wqkv[h % 2]
                w_ready = ldh[h]
                hd_v0 = E.c["v"]
                hd_s0 = E.c["s"]

                def proj(woff, wkey, ti_sin, ti_cos, dstbuf, copy_waits=()):
                    vts = {}
                    rot_s = {}
                    rot_t = {}
                    for tb in range(4):
                        bank = pA if tb % 2 == 0 else pB
                        rbank = pCa if tb % 2 == 0 else pCb
                        tw = [w_ready[wkey], ("v", v_a1_ready)]
                        if tb >= 2:
                            tw.append(("v", vts[tb - 2]))
                        else:
                            tw += [("v", hd_v0), ("s", hd_s0)]
                        for dc in range(DC):
                            E.t(lambda t, dc=dc, tb=tb, bank=bank, slot=slot,
                                woff=woff: t.matmul(
                                bank[:], slot[:, woff + dc, :],
                                a1t(dc, tb * 512, (tb + 1) * 512),
                                start=(dc == 0), stop=(dc == DC - 1)),
                                waits=tw if dc == 0 else ())
                        tmm = E.c["t"]
                        sl_s = tst[:, 2 * (tb % 2), :].rearrange("p (a b) -> p a b", b=128)
                        sl_c = tst[:, 2 * (tb % 2) + 1, :].rearrange("p (a b) -> p a b", b=128)
                        bank3 = bank[:].rearrange("p (a b) -> p a b", b=128)
                        vw = [("t", tmm)] + ld_consts
                        if tb >= 2:
                            vw.append(("t", rot_t[tb - 2]))
                        E.v(lambda v, b3=bank3, sl=sl_s, ti=ti_sin, tb=tb:
                            v.tensor_tensor(
                                sl, b3, tabs[:, ti, 4 * tb:4 * tb + 4, :], ALU.mult),
                            waits=vw)
                        vt = E.v(lambda v, b3=bank3, sl=sl_c, ti=ti_cos, tb=tb:
                                 v.tensor_tensor(
                                     sl, b3, tabs[:, ti, 4 * tb:4 * tb + 4, :], ALU.mult))
                        vts[tb] = vt
                        tw2 = [("v", vt)]
                        if tb >= 2:
                            tw2.append(("s", rot_s[tb - 2]))
                        E.t(lambda t, rb=rbank, tb=tb: t.matmul(
                            rb, rmat[:], tst[:, 2 * (tb % 2), :],
                            start=True, stop=False), waits=tw2)
                        E.t(lambda t, rb=rbank, tb=tb: t.matmul(
                            rb, idbf[:], tst[:, 2 * (tb % 2) + 1, :],
                            start=False, stop=True))
                        rot_t[tb] = E.c["t"]
                        rot_s[tb] = E.s(lambda s, rb=rbank, tb=tb, dst=dstbuf: s.copy(
                            dst[:, tb * 512:(tb + 1) * 512], rb),
                            waits=[("t", rot_t[tb])] +
                                  (list(copy_waits) if tb == 0 else []))
                    return rot_s[3]

                s_q = proj(0, "q", 0, 1, qp, copy_waits=[sp_guard[0]])
                qp3 = qp.rearrange("p (c k) -> p c k", k=128)
                v_qg = E.v(lambda v, h=h: v.tensor_tensor(
                    qp3, qp3, gq[:, h:h + 1, :].broadcast_to([128, NCH, 128]),
                    ALU.mult), waits=[("s", s_q)])
                sp_guard[0] = dmag(lambda g, h=h: g.dma_start(
                    out=qsp_d[i, h], in_=qp), waits=[("v", v_qg)])
                s_k = proj(DC, "k", 2, 3, kp)
                kp3 = kp.rearrange("p (c k) -> p c k", k=128)
                v_kg = E.v(lambda v, h=h: v.tensor_tensor(
                    kp3, kp3, gk[:, h:h + 1, :].broadcast_to([128, NCH, 128]),
                    ALU.mult), waits=[("s", s_k)])
                s_kd_half = {}
                for half in range(2):
                    bank = pb if half == 0 else pb2
                    for j in range(8):
                        E.t(lambda t, half=half, j=j, b=bank: t.transpose(
                            b[:, j * 128:(j + 1) * 128],
                            kp[:, (half * 8 + j) * 128:(half * 8 + j + 1) * 128],
                            idbf[:]),
                            waits=[("v", v_kg)] if j == 0 else ())
                    tkd = E.c["t"]
                    s_kd_half[half] = E.s(lambda s, half=half, b=bank: s.copy(
                        kd3[:, half * 8: half * 8 + 8, :],
                        b[:].rearrange("p (a k) -> p a k", k=128)),
                        waits=[("t", tkd)])
                s_kd = s_kd_half[1]
                s_vst = {}
                t_vtr = {}
                for tb in range(4):
                    bank = pA if tb % 2 == 0 else pB
                    tw = [w_ready["v"]]
                    if tb >= 2:
                        tw.append(("s", s_vst[tb - 2]))
                    else:
                        tw += [("v", E.c["v"]), ("s", E.c["s"])]
                    for dc in range(DC):
                        E.t(lambda t, dc=dc, bank=bank, tb=tb, slot=slot: t.matmul(
                            bank[:], slot[:, 2 * DC + dc, :],
                            a1t(dc, tb * 512, (tb + 1) * 512),
                            start=(dc == 0), stop=(dc == DC - 1)),
                            waits=tw if dc == 0 else ())
                    tmm = E.c["t"]
                    sw = [("t", tmm)]
                    if tb >= 2:
                        sw.append(("t", t_vtr[tb - 2]))
                    s_vst[tb] = E.s(lambda s, bank=bank, tb=tb: s.copy(
                        tst[:, tb % 2, :], bank[:]), waits=sw)
                    bank2 = pb if tb % 2 == 0 else pb2
                    tvw = [("s", s_vst[tb])]
                    if tb >= 2:
                        tvw.append(("s", s_vt3[tb - 2]))
                    for j in range(4):
                        E.t(lambda t, tb=tb, j=j, b2=bank2: t.transpose(
                            b2[:, j * 128:(j + 1) * 128],
                            tst[:, tb % 2, j * 128:(j + 1) * 128], idbf[:]),
                            waits=tvw if j == 0 else ())
                    t_vtr[tb] = E.c["t"]
                    s_vt3[tb] = E.s(lambda s, tb=tb, b2=bank2: s.copy(
                        vt3[:, tb * 4:tb * 4 + 4, :],
                        b2[:, 0:512].rearrange("p (a k) -> p a k", k=128)),
                        waits=[("t", t_vtr[tb])])
                s_v = E.c["s"]
                t_projend = E.c["t"]
                if h + 2 < H:
                    ldh.append(load_head(h + 2))
                import os as _os
                if h > 0 and _os.environ.get("KDBG_SKIPCHAIN"):
                    continue
                v_ms = E.v(lambda v: v.memzero(sigma[:]),
                           waits=[exp_guard[0]])
                s_ms = E.s(lambda s: s.memzero(sbf[:]), waits=[("v", v_ms)])
                vsc, yev, v_sig, s_sbf = {}, {}, {}, {}
                for c in range(NCH):
                    sbank = pA if c % 2 == 0 else pB
                    ybank = yCa if c % 2 == 0 else yCb
                    dbank = pDa if c % 2 == 0 else pDb
                    scs = sc_slots[c % 2]
                    tw = [("v", v_kg), ("s", s_v)]
                    if c >= 2:
                        tw.append(("v", vsc[c - 2]))
                    E.t(lambda t, c=c, b=sbank: t.matmul(
                        b[:, 0:128], kp[:, c * 128:(c + 1) * 128],
                        qp[:, c * 128:(c + 1) * 128], start=True, stop=True),
                        waits=tw)
                    t_sc = E.c["t"]
                    vsc[c] = E.v(lambda v, c=c, b=sbank, scs=scs: v.tensor_tensor(
                        scs, b[:, 0:128], caus[:], ALU.mult),
                        waits=[("t", t_sc)])
                    tw2 = [("v", vsc[c])]
                    if c >= 2:
                        tw2.append(("s", yev[c - 2]))
                    if not (h > 0 and _os.environ.get("KDBG_NOY")):
                        E.t(lambda t, c=c, yb=ybank, scs=scs: t.matmul(
                            yb, scs, vt3[:, c, :],
                            start=True, stop=(c == 0)), waits=tw2)
                        if c > 0:
                            E.t(lambda t, c=c, yb=ybank: t.matmul(
                                yb, qp[:, c * 128:(c + 1) * 128], sbf[:],
                                start=False, stop=True),
                                waits=[("s", s_sbf[c - 1])])
                    t_y = E.c["t"]
                    if h > 0 and _os.environ.get("KDBG_NOYEV"):
                        yev[c] = E.s(lambda s, c=c, yb=ybank: s.copy(
                            sc_slots[0], yb), waits=[("t", t_y)])
                    else:
                        yev[c] = E.s(lambda s, c=c, yb=ybank, h=h: s.copy(
                            a2ch(c, h), yb), waits=[("t", t_y)])
                    tw3 = [("s", s_kd)]
                    if c >= 2:
                        tw3.append(("v", v_sig[c - 2]))
                    E.t(lambda t, c=c, db=dbank: t.matmul(
                        db, kd3[:, c, :], vt3[:, c, :], start=True, stop=True),
                        waits=tw3)
                    t_d = E.c["t"]
                    vw2 = [("t", t_d)]
                    if c > 0:
                        vw2.append(("s", s_sbf[c - 1]))
                    v_sig[c] = E.v(lambda v, c=c, db=dbank, h=h:
                                   v.scalar_tensor_tensor(
                                       sigma[:], sigma[:], g128[h], db,
                                       ALU.mult, ALU.add), waits=vw2)
                    s_sbf[c] = E.s(lambda s, c=c, h=h: s.activation(
                        sbf[:], sigma[:], AF.Copy, scale=cpool[:, 1 + h:2 + h]),
                        waits=[("v", v_sig[c]), ("t", t_y)])
                exp_guard[0] = dmag(lambda g, h=h: g.dma_start(
                    out=cc_in[i].ap()[h], in_=sigma[:]),
                    waits=[("v", v_sig[NCH - 1])])
                if h == 0 and _os.environ.get("KDBG_EARLYDUMP") and stage == "dbgret":
                    E.dma("g", "qO", lambda g: g.dma_start(
                        out=dbg_d["dbgy"].rearrange("p (c k) -> p c k", k=128),
                        in_=a2[:].rearrange("p (c d) -> p c d", c=NCH)[:, :, 0:128]),
                        waits=[("s", E.c["s"])])
            E.op("g", lambda g: g.collective_compute(
                "AllGather", ALU.bypass,
                replica_groups=[[0, 1, 2, 3], [4, 5, 6, 7]],
                ins=[cc_in[i].ap()], outs=[cc_out[i].ap()]),
                waits=E.qfull("qW"), inc="cc", n=1)
            return E.c["cc"], E.c["t"], E.c["s"], E.c["v"]

        def corrections(i, cc_n, t_chain, s_chain):
            vcor = {}
            for h in range(H):
                rel = dmag(lambda g, h=h: g.dma_start(out=qp, in_=qsp_d[i, h]),
                           waits=[("t", E.c["t"])])
                ldm = dmag(lambda g, h=h: g.dma_start(
                    out=mg[:],
                    in_=cc_out[i].ap().rearrange("(r h) p k -> r h p k", h=H)
                        [:, h].rearrange("r p k -> p r k")),
                    waits=[("cc", cc_n)])
                E.v(lambda v, h=h: v.tensor_scalar(
                    sigma[:], mg[:, 0, :], mw[:, h, 0:1], None, ALU.mult),
                    waits=[ldm, ("s", s_chain), exp_guard[0]] + ld_consts)
                for r in range(1, 4):
                    E.v(lambda v, h=h, r=r: v.scalar_tensor_tensor(
                        sigma[:], mg[:, r, :], mw[:, h, r:r + 1], sigma[:],
                        ALU.mult, ALU.add))
                v_si = E.v(lambda v: v.tensor_copy(sinb[:], sigma[:]))
                for c in range(NCH):
                    bank = pA if c % 2 == 0 else pB
                    tw = [("v", v_si), rel]
                    if c >= 2:
                        tw.append(("v", vcor[c - 2]))
                    E.t(lambda t, c=c, b=bank: t.matmul(
                        b[:, 0:128], qp[:, c * 128:(c + 1) * 128], sinb[:],
                        start=True, stop=True), waits=tw)
                    t_c = E.c["t"]
                    vcor[c] = E.v(lambda v, c=c, h=h, b=bank:
                                  v.scalar_tensor_tensor(
                                      a2ch(c, h), b[:, 0:128],
                                      float(gam[h] ** (128 * c)),
                                      a2ch(c, h), ALU.mult, ALU.add),
                                  waits=[("t", t_c)])
            return E.c["v"]

        def gn_and_z(i, v_yfinal):
            ld0 = dmay(lambda y: y.dma_start(out=lnw[:, 0, :], in_=lnw_d[(i, "gnw")]),
                       waits=[("v", v_yfinal)])
            ld1 = dmay(lambda y: y.dma_start(out=lnw[:, 1, :], in_=lnw_d[(i, "gnb")]),
                       waits=[("v", v_yfinal)])
            vgn = {}
            for c in range(NCH):
                bank = pC if c % 2 == 0 else pD
                E.v(lambda v, c=c: v.reduce_sum(
                    gnst[:, 0, c * 8:(c + 1) * 8],
                    a2c(c).rearrange("p (h k) -> p h k", h=H), axis=AX.X),
                    waits=[("v", v_yfinal)] if c == 0 else ())
                vw = []
                if c >= 2:
                    vw.append(("v", vgn[c - 2]))
                E.v(lambda v, c=c, b=bank: v.tensor_mul(
                    b[:], a2c(c), a2c(c)), waits=vw)
                vgn[c] = E.v(lambda v, c=c, b=bank: v.reduce_sum(
                    gnst[:, 1, c * 8:(c + 1) * 8],
                    b[:].rearrange("p (h k) -> p h k", h=H), axis=AX.X))
            E.v(lambda v: v.tensor_scalar(
                gnst[:, 4, :], gnst[:, 0, :], 1.0 / HD, None, ALU.mult))
            E.v(lambda v: v.tensor_scalar(
                gnst[:, 2, :], gnst[:, 1, :], 1.0 / HD, None, ALU.mult))
            E.v(lambda v: v.tensor_mul(gnst[:, 5, :], gnst[:, 4, :], gnst[:, 4, :]))
            E.v(lambda v: v.tensor_sub(gnst[:, 2, :], gnst[:, 2, :], gnst[:, 5, :]))
            v_var = E.c["v"]
            s_std = E.s(lambda s: s.activation(
                gnst[:, 2, :], gnst[:, 2, :], AF.Sqrt, bias=cpool[:, 0:1]),
                waits=[("v", v_var)])
            E.v(lambda v: v.reciprocal(gnst[:, 2, :], gnst[:, 2, :]),
                waits=[("s", s_std)])
            v_c0 = E.v(lambda v: v.scalar_tensor_tensor(
                gnst[:, 3, :], gnst[:, 4, :], -1.0, gnst[:, 2, :],
                ALU.mult, ALU.mult))
            wg_full = hbuf[:, 0:8192].rearrange("p (d x) -> p d x", d=DC)
            ld_wg = dmag(lambda g: g.dma_start(out=wg_full, in_=wg_d[i]),
                         waits=[("t", E.c["t"]), ("v", v_yfinal)])
            gt0 = hbuf[:, 8192:8704]
            gt1 = hbuf[:, 8704:9216]
            szT = {}
            tgz = {}
            for c in range(NCH):
                for h in range(H):
                    E.v(lambda v, c=c, h=h: v.tensor_scalar(
                        a2ch(c, h), a2ch(c, h),
                        gnst[:, 2, c * 8 + h:c * 8 + h + 1],
                        gnst[:, 3, c * 8 + h:c * 8 + h + 1],
                        ALU.mult, ALU.add),
                        waits=[("v", v_c0)] if (c == 0 and h == 0) else ())
                E.v(lambda v, c=c: v.tensor_tensor(
                    a2c(c), a2c(c), lnw[:, 0, :], ALU.mult),
                    waits=[ld0] if c == 0 else ())
                v_yn = E.v(lambda v, c=c: v.tensor_tensor(
                    a2c(c), a2c(c), lnw[:, 1, :], ALU.add),
                    waits=[ld1] if c == 0 else ())
                for half in range(2):
                    bank = pCa2 if half == 0 else pDa2
                    gts = gt0 if half == 0 else gt1
                    tw = [ld_wg, ("v", v_c0)]
                    if c >= 1:
                        tw.append(("v", vz_done[0]))
                    for dc in range(DC):
                        E.t(lambda t, c=c, dc=dc, half=half, b=bank: t.matmul(
                            b, a1t(dc, c * 128, (c + 1) * 128),
                            wg_full[:, dc, half * 512:(half + 1) * 512],
                            start=(dc == 0), stop=(dc == DC - 1)),
                            waits=tw if dc == 0 else ())
                    t_g = E.c["t"]
                    sg = E.s(lambda s, b=bank, gts=gts: s.activation(
                        gts, b, AF.Silu), waits=[("t", t_g)])
                    vz_done[0] = E.v(lambda v, c=c, half=half, gts=gts:
                                     v.tensor_tensor(
                                         a2[:, c * D + half * 512:
                                            c * D + (half + 1) * 512],
                                         a2[:, c * D + half * 512:
                                            c * D + (half + 1) * 512],
                                         gts, ALU.mult),
                                     waits=[("s", sg), ("v", v_yn)])
                v_z = E.c["v"]
                bank2 = pb if c % 2 == 0 else pb2
                tzw = [("v", v_z)]
                if c >= 2:
                    tzw.append(("s", szT[c - 2]))
                for j in range(DC):
                    E.t(lambda t, c=c, j=j, b2=bank2: t.transpose(
                        b2[:, j * 128:(j + 1) * 128],
                        a2[:, c * D + j * 128:c * D + (j + 1) * 128], idbf[:]),
                        waits=tzw if j == 0 else ())
                tgz[c] = E.c["t"]
                szT[c] = E.s(lambda s, c=c, b2=bank2: s.copy(
                    a1[:].rearrange("p (d t) -> p d t", d=DC)
                    [:, :, c * 128:(c + 1) * 128],
                    b2[:].rearrange("p (d t) -> p d t", d=DC)),
                    waits=[("t", tgz[c])])
            return E.c["s"], E.c["t"]

        def wo_residual(i, s_zT, t_gz):
            wo_full = hbuf[:, 0:8192].rearrange("p (d x) -> p d x", d=DC)
            ld = dmag(lambda g: g.dma_start(out=wo_full, in_=wo_d[i]),
                      waits=[("t", t_gz)])
            vwo = {}
            for c in range(NCH):
                bank = pC if c % 2 == 0 else pD
                tw = [ld, ("s", s_zT)]
                if c >= 2:
                    tw.append(("v", vwo[c - 2]))
                for dc in range(DC):
                    for dh in range(2):
                        E.t(lambda t, c=c, dc=dc, dh=dh, b=bank: t.matmul(
                            b[:, dh * 512:(dh + 1) * 512],
                            a1t(dc, c * 128, (c + 1) * 128),
                            wo_full[:, dc, dh * 512:(dh + 1) * 512],
                            start=(dc == 0), stop=(dc == DC - 1)),
                            waits=tw if (dc == 0 and dh == 0) else ())
                t_w = E.c["t"]
                vwo[c] = E.v(lambda v, c=c, b=bank: v.tensor_tensor(
                    xres3[:, c, :], b[:], xres3[:, c, :], ALU.add),
                    waits=[("t", t_w)])
            return E.c["v"], E.c["t"]

        def ffn(i, v_a1_ready, strips_d, groups_d, bpp_slot, t_free_hbuf,
                lora=False, init_copy=False, bias_dram=None):
            w2gv = hbuf[:, 0:8192].rearrange("p (f d) -> p f d", f=8)
            wstv = hbuf[:, 8192:10240].rearrange("p (a d k) -> p a d k", a=2, d=DC)
            ldb = dmay(lambda y: y.dma_start(
                out=bpp[:, bpp_slot, :],
                in_=lnw_d[(i, "b1pp")] if bias_dram is None else bias_dram))
            sev = {}
            vff = {}
            lds = {}
            act = AF.Relu if lora else AF.Silu
            t_guard = t_free_hbuf
            for g in range(4):
                ldg = dmag(lambda g_, g=g: g_.dma_start(
                    out=w2gv, in_=groups_d[g]),
                    waits=[("t", max(t_guard, E.c["t"] if g > 0 else 0))])
                if lora:
                    ld_bcat[0] = dmag(lambda g_, g=g: g_.dma_start(
                        out=bcat_sb[0:32, g % 2, :],
                        in_=bcat_d[0:32, g * 1024:(g + 1) * 1024]),
                        waits=[("t", E.c["t"])])
                for fcg in range(8):
                    fc = g * 8 + fcg
                    lds[fc] = dmag(lambda g_, fc=fc: g_.dma_start(
                        out=wstv[:, fc % 2], in_=strips_d[fc]),
                        waits=[("t", E.c["t"])])
                    for tb in range(4):
                        bank = pA if tb % 2 == 0 else pB
                        tw = [lds[fc], ("v", v_a1_ready)]
                        key = fc * 4 + tb
                        if key >= 2:
                            tw.append(("s", sev[key - 2]))
                        for dc in range(DC):
                            E.t(lambda t, dc=dc, fc=fc, tb=tb, b=bank: t.matmul(
                                b[:], wstv[:, fc % 2, dc, :],
                                a1t(dc, tb * 512, (tb + 1) * 512),
                                start=(dc == 0),
                                stop=(dc == DC - 1 and not lora)),
                                waits=tw if dc == 0 else ())
                        if lora:
                            E.t(lambda t, fc=fc, tb=tb, b=bank, g=g,
                                fcg=fcg: t.matmul(
                                b[:], bcat_sb[0:32, g % 2,
                                              fcg * 128:(fcg + 1) * 128],
                                umt[0:32, tb * 512:(tb + 1) * 512],
                                start=False, stop=True),
                                waits=[("v", v_umt[0]), ld_bcat[0]])
                        t_m = E.c["t"]
                        sev[key] = E.s(
                            lambda s, fcg=fcg, tb=tb, b=bank, fc=fc:
                            s.activation(
                                h1sl(fcg, tb * 512, (tb + 1) * 512), b[:],
                                act, bias=bpp[:, bpp_slot, fc:fc + 1]),
                            waits=[("t", t_m), ldb])
                s_h1 = E.c["s"]
                for c in range(NCH):
                    bank = pC if c % 2 == 0 else pD
                    tw = [ldg, ("s", s_h1)]
                    if c >= 2:
                        tw.append(("v", vff[(g, c - 2)]))
                    elif g > 0:
                        tw.append(("v", vff[(g - 1, c + 14)]))
                    for fcg in range(8):
                        for dh in range(2):
                            E.t(lambda t, c=c, fcg=fcg, dh=dh, b=bank: t.matmul(
                                b[:, dh * 512:(dh + 1) * 512],
                                h1sl(fcg, c * 128, (c + 1) * 128),
                                w2gv[:, fcg, dh * 512:(dh + 1) * 512],
                                start=(fcg == 0), stop=(fcg == 7)),
                                waits=tw if (fcg == 0 and dh == 0) else ())
                    t_w2 = E.c["t"]
                    if init_copy and g == 0:
                        vff[(g, c)] = E.v(lambda v, c=c, b=bank: v.tensor_copy(
                            xres3[:, c, :], b[:]),
                            waits=[("t", t_w2), ("v", v_xres_free[0])] +
                                  x2sp_guard[0])
                    else:
                        vff[(g, c)] = E.v(lambda v, c=c, b=bank: v.tensor_tensor(
                            xres3[:, c, :], b[:], xres3[:, c, :], ALU.add),
                            waits=[("t", t_w2)])
                t_guard = 0
            return E.c["v"], E.c["t"]

        def add_bias_big(v_wait, dram_ap):
            ld = dmay(lambda y: y.dma_start(out=lnw[:, 0, :], in_=dram_ap),
                      waits=[("v", v_wait)])
            return E.v(lambda v: v.tensor_tensor(
                xres3[:, :, :], xres3[:, :, :],
                lnw[:, 0:1, :].broadcast_to([128, NCH, D]), ALU.add),
                waits=[ld, ("v", v_wait)])

        exp_guard = [("cc", 0)]
        sp_guard = [("cc", 0)]
        vz_done = [0]
        v_xres_free = [0]
        x2sp_guard = [0]
        sc_slots = None
        pCa2, pDa2 = pC[:, 0:512], pD[:, 0:512]
        s_vt3 = {}
        v_umt = [0]
        ld_bcat = [0]

        v_prev_v = v_m0
        if stage == "dbgret":
            sc_slots = [tst[:, 2, 0:128], tst[:, 2, 128:256]]
            v_a1 = ln_to_a1(0, "ln1w", "ln1b", v_prev_v, E.c["s"], E.c["t"])
            Hsave = globals()["H"]
            import os
            globals()["H"] = int(os.environ.get("KDBG_H", "1"))
            cc_n, t_ch, s_ch, v_ch = retention(0, v_a1)
            globals()["H"] = Hsave
            for nm, src_ap in (("dbgq", qp), ("dbgk", kp),
                               ("dbgv", hbuf[:, 6144:8192]),
                               ("dbgs", None)):
                if nm == "dbgs":
                    E.dma("g", "qO", lambda g: g.dma_start(
                        out=dbg_d["dbgs"][:, 0:128], in_=sigma[:]),
                        waits=[("v", E.c["v"]), ("s", E.c["s"]),
                               ("t", E.c["t"])])
                else:
                    E.dma("g", "qO", lambda g, nm=nm, sa=src_ap: g.dma_start(
                        out=dbg_d[nm], in_=sa),
                        waits=[("v", E.c["v"]), ("s", E.c["s"]),
                               ("t", E.c["t"])])
            import os as _os2
            if not _os2.environ.get("KDBG_EARLYDUMP"):
                E.dma("g", "qO", lambda g: g.dma_start(
                    out=dbg_d["dbgy"].rearrange("p (c k) -> p c k", k=128),
                    in_=a2[:].rearrange("p (c d) -> p c d", c=NCH)[:, :, 0:128]),
                    waits=[("s", E.c["s"])])
            E.dma("g", "qO", lambda g: g.dma_start(
                out=dbg_d["dbga"], in_=a2[:, 0:8192]),
                waits=[("s", E.c["s"])])
            E.dma("g", "qO", lambda g: g.dma_start(
                out=dbg_d["dbgb"], in_=a2[:, 8192:16384]),
                waits=[("s", E.c["s"])])
            E.op("g", None, waits=E.qfull("qO"))

        if stage_i >= _ORDER.index("yraw0"):
            sc_slots = [tst[:, 2, 0:128], tst[:, 2, 128:256]]
            for i in range(NL):
                s_vt3 = {}
                v_a1 = ln_to_a1(i, "ln1w", "ln1b", v_prev_v, E.c["s"], E.c["t"])
                cc_n, t_ch, s_ch, v_ch = retention(i, v_a1)
                v_yfin = corrections(i, cc_n, t_ch, s_ch)
                if stage == ("yraw0" if i == 0 else "yraw1"):
                    tap_bf16(a2, v_yfin, E.c["s"])
                    break
                s_zT, t_gz = gn_and_z(i, v_yfin)
                v_xr, t_wo = wo_residual(i, s_zT, t_gz)
                if stage == ("wo0" if i == 0 else "wo1"):
                    tap_f32(xres3, v_xr)
                    break
                v_a1b = ln_to_a1(i, "ln2w", "ln2b", v_xr, E.c["s"], E.c["t"])
                v_ff, t_ff = ffn(i, v_a1b, w1_d[i], w2_d[i], 0, t_wo)
                v_prev_v = add_bias_big(v_ff, lnw_d[(i, "b2bc")])
                if stage == ("ffn0" if i == 0 else "ffn1"):
                    tap_f32(xres3, v_prev_v)
                    break

        if stage_i >= _ORDER.index("x2"):
            es2.close()
            rbuf = sb("rbuf", [128, 2, D], F32)
            obuf = sb("obuf", [128, 2, D], F32)
            logits = sb("logits", [128, NCH, 8], F32)
            msk8 = sb("msk8", [128, NCH, 8], F32)
            m32 = sb("m32", [128, NCH, 8, 4], BF16)
            um = sb("um", [128, NCH, 32], BF16)
            umt = sb("umt", [32, T], BF16)
            acat_sb = sb("acatsb", [128, DC, 32], BF16)
            bcat_sb = sb("bcatsb", [32, 2, 1024], BF16)
            gw_sb = sb("gwsb", [128, DC, 8], F32)
            gb_sb = sb("gbsb", [128, 8], F32)
            x2t = sb("x2t", [128, DC, 128], F32)

            ld_small = []
            for ap_, dst in [(gw_d, gw_sb), (gateb_d, gb_sb)]:
                ld_small.append(dmay(lambda y, a=ap_, d=dst: y.dma_start(
                    out=d[:], in_=a), waits=[("v", E.c["v"]), ("t", E.c["t"])]))
            ld_acat = dmag(lambda g: g.dma_start(out=acat_sb[:], in_=acat_d),
                           waits=[("t", E.c["t"])])

            vr = ln_stats(xres3, wait_v=[("v", v_prev_v)],
                          wait_s=[("v", v_prev_v)])
            vx2 = {}
            for c in range(NCH):
                rl = dmas(lambda s, c=c: s.dma_start(
                    out=rbuf[:, c % 2, :], in_=x0_d[c]),
                    waits=([("v", vr)] if c < 2 else [("v", vx2[c - 2])]))
                E.v(lambda v, c=c: v.tensor_scalar(
                    obuf[:, c % 2, :], xres3[:, c, :],
                    st_mean[:, c:c + 1], st_rstd[:, c:c + 1],
                    ALU.subtract, ALU.mult),
                    waits=([("v", vr)] if c == 0 else []) +
                          ([("v", vx2[c - 2])] if c >= 2 else []))
                vx2[c] = E.v(lambda v, c=c: v.tensor_tensor(
                    xres3[:, c, :], obuf[:, c % 2, :], rbuf[:, c % 2, :],
                    ALU.add), waits=[rl])
                dmas(lambda s, c=c: s.dma_start(
                    out=x2_d[c], in_=xres3[:, c, :]), waits=[("v", vx2[c])])
            s_x2spill = E.qfull("qS")
            v_x2 = E.c["v"]
            if stage == "x2":
                tap_f32(xres3, v_x2)
            else:
                vr2 = ln_stats(xres3, wait_v=[("v", v_x2)],
                               wait_s=[("v", v_x2)])
                for c in range(NCH):
                    E.v(lambda v, c=c: v.tensor_scalar(
                        a2[:, c * D:(c + 1) * D], xres3[:, c, :],
                        st_mean[:, c:c + 1], st_rstd[:, c:c + 1],
                        ALU.subtract, ALU.mult),
                        waits=[("v", vr2)] if c == 0 else ())
                v_stage2 = E.c["v"]
                v_a1m = transpose_block_to_a1(
                    lambda c, j: a2[:, c * D + j * 128: c * D + (j + 1) * 128],
                    lambda c: v_stage2, None, extra_t_waits=[("t", E.c["t"])])
                vgt = {}
                sgt = {}
                for c in range(NCH):
                    E.v(lambda v, c=c: v.tensor_scalar(
                        obuf[:, c % 2, :], xres3[:, c, :],
                        st_mean[:, c:c + 1], st_rstd[:, c:c + 1],
                        ALU.subtract, ALU.mult),
                        waits=([("v", vr2)] if c == 0 else []) +
                              ([("v", vgt[c - 2])] if c >= 2 else []))
                    v_ob = E.c["v"]
                    tw = [("v", v_ob)]
                    if c >= 2:
                        tw.append(("v", vgt[c - 2]))
                    for j in range(DC):
                        E.t(lambda t, c=c, j=j: t.transpose(
                            pC[:, j * 128:(j + 1) * 128],
                            obuf[:, c % 2, j * 128:(j + 1) * 128], idf32[:]),
                            waits=tw if j == 0 else ())
                    t_tr = E.c["t"]
                    vgt[c] = E.v(lambda v, c=c: v.tensor_copy(
                        x2t[:].rearrange("p d k -> p (d k)"), pC[:]),
                        waits=[("t", t_tr)])
                    for dc in range(DC):
                        E.t(lambda t, c=c, dc=dc: t.matmul(
                            pD[:, c * 8: (c + 1) * 8],
                            x2t[:, dc, :], gw_sb[:, dc, :],
                            start=(dc == 0), stop=(dc == DC - 1)),
                            waits=[("v", vgt[c]), ld_small[0],
                                   ("t", E.c["t"])] if dc == 0 else ())
                    sgt[c] = E.c["t"]
                t_gate = E.c["t"]
                v_lg = E.v(lambda v: v.tensor_tensor(
                    logits[:, :, :],
                    pD[:, 0:128].rearrange("p (c e) -> p c e", e=8),
                    gb_sb[:, 0:8].rearrange("p (a e) -> p a e", a=1)
                    .broadcast_to([128, NCH, 8]), ALU.add),
                    waits=[("t", t_gate), ld_small[1]])
                for c in range(NCH):
                    E.v(lambda v, c=c: v.reduce_max(
                        st_mx[:], logits[:, c, :], axis=AX.X),
                        waits=[("v", v_lg)] if c == 0 else ())
                    E.v(lambda v, c=c: v.tensor_scalar(
                        msk8[:, c, :], logits[:, c, :], st_mx[:], None,
                        ALU.is_equal))
                v_msk = E.c["v"]
                for r in range(4):
                    E.v(lambda v, r=r: v.tensor_copy(
                        m32[:, :, :, r], msk8[:]), waits=[("v", v_msk)])
                v_m32 = E.c["v"]
                sumt = {}
                for c in range(NCH):
                    tw = [("v", v_a1m), ld_acat]
                    if c >= 2:
                        tw.append(("v", sumt[c - 2][0]))
                    for dc in range(DC):
                        E.t(lambda t, c=c, dc=dc: t.matmul(
                            pA[:, 0:32], a1t(dc, c * 128, (c + 1) * 128),
                            acat_sb[:, dc, :],
                            start=(dc == 0), stop=(dc == DC - 1)),
                            waits=tw if dc == 0 else ())
                    t_u = E.c["t"]
                    vu = E.v(lambda v, c=c: v.tensor_tensor(
                        um[:, c, :], pA[:, 0:32],
                        m32[:, c, :, :].rearrange("p e r -> p (e r)"),
                        ALU.mult), waits=[("t", t_u), ("v", v_m32)])
                    bank2 = pb if c % 2 == 0 else pb2
                    tw2 = [("v", vu)]
                    if c >= 2:
                        tw2.append(("s", sumt[c - 2][1]))
                    E.t(lambda t, c=c, b2=bank2: t.transpose(
                        b2[0:32, 0:128], um[:, c, :], idbf[:]),
                        waits=tw2)
                    t_ut = E.c["t"]
                    s_ut = E.s(lambda s, c=c, b2=bank2: s.copy(
                        umt[0:32, c * 128:(c + 1) * 128], b2[0:32, 0:128]),
                        waits=[("t", t_ut)])
                    sumt[c] = (vu, s_ut)
                v_umt[0] = E.c["v"]
                s_umt = E.c["s"]
                v_xres_free[0] = E.c["v"]
                x2sp_guard[0] = s_x2spill
                v_F, t_F = ffn(None, v_a1m, wi_d, wom_d, 1, E.c["t"],
                               lora=True, init_copy=True, bias_dram=wib_d)
                v_F = add_bias_big(v_F, wob_d)
                vr3 = ln_stats(xres3, wait_v=[("v", v_F)], wait_s=[("v", v_F)])
                vfin = {}
                qo_c = {}
                for c in range(NCH):
                    rl = dmas(lambda s, c=c: s.dma_start(
                        out=rbuf[:, c % 2, :], in_=x2_d[c]),
                        waits=(([("v", vr3)] + s_x2spill) if c < 2
                               else [("v", vfin[c - 2])]))
                    E.v(lambda v, c=c: v.tensor_scalar(
                        obuf[:, c % 2, :], xres3[:, c, :],
                        st_mean[:, c:c + 1], st_rstd[:, c:c + 1],
                        ALU.subtract, ALU.mult),
                        waits=([("v", vr3)] if c == 0 else []) +
                              ([("v", vfin[c - 2]), qo_c[c - 2]]
                               if c >= 2 else []))
                    vfin[c] = E.v(lambda v, c=c: v.tensor_tensor(
                        obuf[:, c % 2, :], obuf[:, c % 2, :],
                        rbuf[:, c % 2, :], ALU.add), waits=[rl])
                    qo_c[c] = dmao(lambda s, c=c: s.dma_start(
                        out=y_d[c], in_=obuf[:, c % 2, :]),
                        waits=[("v", vfin[c]), ("s", E.c["s"])])
                E.op("s", None, waits=E.qfull("qO"))

        block = es.enter_context(nc.Block())

        def replay(lst):
            def run(eng):
                for waits, fn, inc, n in lst:
                    for snm, val in waits:
                        eng.wait_ge(sems[snm], val)
                    if fn is None:
                        continue
                    ins = fn(eng)
                    if inc is not None:
                        ins.then_inc(sems[inc], n)
            return run

        block.tensor(replay(E.q["t"]))
        block.vector(replay(E.q["v"]))
        block.scalar(replay(E.q["s"]))
        block.gpsimd(replay(E.q["g"]))
        block.sync(replay(E.q["y"]))

    return nc


def _prep_inputs(inputs):
    X = np.asarray(inputs["L0_attn_X"], np.float32)
    gam = _gammas()

    base = {}
    for i in range(NL):
        for nm, key in [("wq", "L0_attn_WQ"), ("wk", "L0_attn_WK"), ("wv", "L0_attn_WV")]:
            w = np.asarray(inputs[key][i], np.float32)
            base[f"{nm}{i}"] = np.ascontiguousarray(
                w.reshape(H, DC, 128, HD).transpose(0, 2, 1, 3))
        for nm, key in [("wg", "L0_attn_WG"), ("wo", "L0_attn_WO")]:
            w = np.asarray(inputs[key][i], np.float32)
            base[f"{nm}{i}"] = np.ascontiguousarray(
                w.reshape(DC, 128, D).transpose(1, 0, 2))
        w1 = np.asarray(inputs["L0_attn_ffn_w1"][i], np.float32)
        base[f"w1_{i}"] = np.ascontiguousarray(
            w1.reshape(DC, 128, FC, 128).transpose(2, 1, 0, 3))
        w2 = np.asarray(inputs["L0_attn_ffn_w2"][i], np.float32)
        base[f"w2_{i}"] = np.ascontiguousarray(
            w2.reshape(4, 8, 128, D).transpose(0, 2, 1, 3))
        for nm, key in [("ln1w", "L0_attn_ln1_w"), ("ln1b", "L0_attn_ln1_b"),
                        ("ln2w", "L0_attn_ln2_w"), ("ln2b", "L0_attn_ln2_b"),
                        ("gnw", "L0_attn_gn_w"), ("gnb", "L0_attn_gn_b")]:
            vec = np.asarray(inputs[key][i], np.float32)
            base[f"{nm}{i}"] = np.ascontiguousarray(
                np.broadcast_to(vec[None, :], (128, D)))
        b1 = np.asarray(inputs["L0_attn_ffn_b1"][i], np.float32)
        base[f"b1pp{i}"] = np.ascontiguousarray(b1.reshape(FC, 128).T)
        b2 = np.asarray(inputs["L0_attn_ffn_b2"][i], np.float32)
        base[f"b2bc{i}"] = np.ascontiguousarray(np.broadcast_to(b2[None, :], (128, D)))

    wi = np.asarray(inputs["L0_ffn_wi_W"], np.float32)
    wiT = np.ascontiguousarray(wi.T)
    base["wiT"] = np.ascontiguousarray(
        wiT.reshape(DC, 128, FC, 128).transpose(2, 1, 0, 3))
    wo = np.asarray(inputs["L0_ffn_wo_W"], np.float32)
    woT = np.ascontiguousarray(wo.T)
    base["woT"] = np.ascontiguousarray(
        woT.reshape(4, 8, 128, D).transpose(0, 2, 1, 3))
    gw = np.asarray(inputs["L0_ffn_gate_W"], np.float32)
    base["gwT"] = np.ascontiguousarray(gw.T.reshape(DC, 128, 8).transpose(1, 0, 2))
    la = np.asarray(inputs["L0_ffn_lora_A"], np.float32)
    base["acatT"] = np.ascontiguousarray(
        la.reshape(32, D).T.reshape(DC, 128, 32).transpose(1, 0, 2))
    lb = np.asarray(inputs["L0_ffn_lora_B"], np.float32)
    base["bcat"] = np.ascontiguousarray(lb.transpose(0, 2, 1).reshape(32, FF))
    base["gatebbc"] = np.ascontiguousarray(np.broadcast_to(
        np.asarray(inputs["L0_ffn_gate_b"], np.float32)[None, :], (128, 8)))
    base["wibpp"] = np.ascontiguousarray(
        np.asarray(inputs["L0_ffn_wi_b"], np.float32).reshape(FC, 128).T)
    base["wobbc"] = np.ascontiguousarray(np.broadcast_to(
        np.asarray(inputs["L0_ffn_wo_b"], np.float32)[None, :], (128, D)))

    sinq, cosq, sink, cosk = _xpos_tables()
    u = np.arange(128, dtype=np.float64)
    caus = (u[None, :] >= u[:, None]).astype(np.float32)
    gqv = np.stack([g ** (u + 1) for g in gam], 0)
    gkv = np.stack([g ** (-(u + 1)) for g in gam], 0)
    gq_t = np.ascontiguousarray(np.broadcast_to(gqv[None], (128, H, 128))).astype(np.float32)
    gk_t = np.ascontiguousarray(np.broadcast_to(gkv[None], (128, H, 128))).astype(np.float32)
    ident = np.eye(128, dtype=np.float32)
    rmt = np.zeros((128, 128), np.float32)
    for j in range(64):
        rmt[2 * j + 1, 2 * j] = -1.0
        rmt[2 * j, 2 * j + 1] = 1.0
    cpool = np.zeros((128, 12), np.float32)
    cpool[:, 0] = EPS
    for h in range(H):
        cpool[:, 1 + h] = np.float64(gam[h]) ** 128
    cpool[:, 9] = 1.0

    maps = []
    for core in range(N_CORES):
        b, q = core // 4, core % 4
        m = dict(base)
        sl = slice(q * T, (q + 1) * T)
        m["x0"] = np.ascontiguousarray(X[b, sl].reshape(NCH, 128, D))
        tb = np.stack([
            np.ascontiguousarray(t[sl].astype(np.float32).T.reshape(128, NCH, 128))
            for t in (sinq, cosq, sink, cosk)], 0)
        m["tabs"] = np.ascontiguousarray(tb)
        m["caus"] = caus
        m["gq"] = gq_t
        m["gk"] = gk_t
        m["identbf"] = ident.astype(BF)
        m["identf32"] = ident
        m["rmat"] = rmt.astype(BF)
        m["cpool"] = cpool
        mwa = np.zeros((H, 4), np.float64)
        for h, g in enumerate(gam):
            for cp in range(4):
                if cp < q:
                    mwa[h, cp] = np.float64(g) ** (2048 * (q - 1 - cp) + 128)
        m["mw"] = np.ascontiguousarray(
            np.broadcast_to(mwa.astype(np.float32)[None], (128, H, 4)))
        maps.append(m)
    return maps


_NC_CACHE = {}


def kernel(**inputs):
    maps = _prep_inputs(inputs)
    if STAGE not in _NC_CACHE:
        _NC_CACHE[STAGE] = build_program(STAGE)
    res = run_bass_kernel_spmd(_NC_CACHE[STAGE], maps, core_ids=list(range(N_CORES)))
    out = np.empty((2, S_FULL, D), np.float32)
    for core in range(N_CORES):
        b, q = core // 4, core % 4
        out[b, q * T:(q + 1) * T] = res.results[core]["y"].reshape(T, D)
    return out


# version 03 (final)
# speedup vs baseline: 1.0239x; 1.0239x over previous
import contextlib

import numpy as np
import ml_dtypes

import concourse.bass as bass
import concourse.mybir as mybir
from concourse.bass_utils import run_bass_kernel_spmd

F32 = mybir.dt.float32
BF16 = mybir.dt.bfloat16
AF = mybir.ActivationFunctionType
ALU = mybir.AluOpType
AX = mybir.AxisListType
BF = ml_dtypes.bfloat16

T = 2048
NCH = 16
D = 1024
DC = 8
H = 8
HD = 128
FF = 4096
FC = 32
NL = 2
EPS = 1e-5
S_FULL = 8192
N_CORES = 8

STAGE = "final"
_ORDER = ["echo", "stats", "m0", "dbgret", "yraw0", "wo0", "ffn0",
          "yraw1", "wo1", "ffn1", "x2", "final"]


def _gammas():
    return 1.0 - np.exp(np.linspace(np.log(1.0 / 32), np.log(1.0 / 512), H))


def _xpos_tables():
    scale_vec = (np.arange(0, HD, 2) + 0.4 * HD) / (1.4 * HD)
    power = (np.arange(S_FULL, dtype=np.float64) / 512)[:, None]
    scale = scale_vec[None, :] ** power
    half = HD // 2
    inv_freq = 1.0 / 10000.0 ** (np.arange(half, dtype=np.float64) / half)
    ang = np.arange(S_FULL, dtype=np.float64)[:, None] * inv_freq[None, :]
    sin, cos = np.sin(ang), np.cos(ang)
    dup = lambda m: np.repeat(m, 2, axis=-1)
    return (dup(sin * scale), dup(cos * scale), dup(sin / scale), dup(cos / scale))


class Emit:
    ENGS = ("t", "v", "s", "g", "y")

    def __init__(self):
        self.q = {e: [] for e in self.ENGS}
        self.c = {}

    def sem(self, name):
        self.c[name] = 0

    def op(self, eng, fn, waits=(), inc=None, n=1):
        for s, v in waits:
            assert v <= self.c[s], f"wait {s}>={v} but only {self.c[s]} emitted"
        if inc is not None:
            assert fn is not None
            self.c[inc] += n
        self.q[eng].append((tuple(waits), fn, inc, n))
        return self.c.get(inc)

    def qsems(self, queue, depth):
        names = [f"{queue}{j}" for j in range(depth)]
        for n in names:
            self.sem(n)
        self.qpool = getattr(self, "qpool", {})
        self.qpool[queue] = {"names": names, "idx": 0}
        return names

    def dma(self, eng, queue, fn, waits=()):
        pool = self.qpool[queue]
        sem = pool["names"][pool["idx"]]
        pool["idx"] = (pool["idx"] + 1) % len(pool["names"])
        w = [(sem, self.c[sem])] + list(waits)
        self.op(eng, fn, w, sem, 16)
        return (sem, self.c[sem])

    def qfull(self, queue):
        return [(n, self.c[n]) for n in self.qpool[queue]["names"]]

    def t(self, fn, waits=()):
        return self.op("t", fn, waits, "t", 1)

    def v(self, fn, waits=()):
        return self.op("v", fn, [("v", self.c["v"])] + list(waits), "v", 1)

    def s(self, fn, waits=()):
        return self.op("s", fn, [("s", self.c["s"])] + list(waits), "s", 1)


def build_program(stage):
    nc = bass.Bass("TRN2", target_bir_lowering=False, debug=False)
    E = Emit()
    stage_i = _ORDER.index(stage)

    gam = _gammas()
    g128 = [float(g) ** 128 for g in gam]

    def din(name, shape, dt=F32):
        return nc.dram_tensor(name, shape, dt, kind="ExternalInput").ap()

    x0_d = din("x0", [NCH, 128, D])
    y_d = nc.dram_tensor("y", [NCH, 128, D], F32, kind="ExternalOutput").ap()

    wq_d, wk_d, wv_d, wg_d, wo_d, w1_d, w2_d = [], [], [], [], [], [], []
    lnw_d = {}
    for i in range(NL):
        wq_d.append(din(f"wq{i}", [H, 128, DC, HD]))
        wk_d.append(din(f"wk{i}", [H, 128, DC, HD]))
        wv_d.append(din(f"wv{i}", [H, 128, DC, HD]))
        wg_d.append(din(f"wg{i}", [128, DC, D]))
        wo_d.append(din(f"wo{i}", [128, DC, D]))
        w1_d.append(din(f"w1_{i}", [FC, 128, DC, 128]))
        w2_d.append(din(f"w2_{i}", [4, 128, 8, D]))
        for nm in ("ln1w", "ln1b", "ln2w", "ln2b", "gnw", "gnb", "b2bc"):
            lnw_d[(i, nm)] = din(f"{nm}{i}", [128, D])
        lnw_d[(i, "b1pp")] = din(f"b1pp{i}", [128, FC])
    wi_d = din("wiT", [FC, 128, DC, 128])
    wom_d = din("woT", [4, 128, 8, D])
    gw_d = din("gwT", [128, DC, 8])
    acat_d = din("acatT", [128, DC, 32])
    bcat_d = din("bcat", [32, FF])
    gateb_d = din("gatebbc", [128, 8])
    wib_d = din("wibpp", [128, FC])
    wob_d = din("wobbc", [128, D])
    tabs_d = din("tabs", [4, 128, NCH, 128])
    caus_d = din("caus", [128, 128])
    gq_d = din("gq", [128, H, 128])
    gk_d = din("gk", [128, H, 128])
    id_d = din("identbf", [128, 128], BF16)
    idf_d = din("identf32", [128, 128])
    rm_d = din("rmat", [128, 128], BF16)
    mw_d = din("mw", [128, H, 4])
    cpool_d = din("cpool", [128, 12])

    dbg_d = {n: nc.dram_tensor(n, [128, T], F32, kind="ExternalOutput").ap()
             for n in ("dbgq", "dbgk", "dbgv", "dbgy", "dbgs")} if stage == "dbgret" else {}
    if stage == "dbgret":
        dbg_d["dbga"] = nc.dram_tensor("dbga", [128, 8192], F32, kind="ExternalOutput").ap()
        dbg_d["dbgb"] = nc.dram_tensor("dbgb", [128, 8192], F32, kind="ExternalOutput").ap()
    x2_d = nc.dram_tensor("x2spill", [NCH, 128, D], F32).ap()
    qsp_d = nc.dram_tensor("qspill", [NL, H, 128, T], BF16).ap()
    cc_in = [nc.dram_tensor(f"ccin{i}", [H, 128, HD], F32) for i in range(NL)]
    cc_out = [nc.dram_tensor(f"ccout{i}", [4 * H, 128, HD], F32) for i in range(NL)]

    sem_names = ["t", "v", "s", "g", "cc"]
    for sname in sem_names:
        E.sem(sname)
    sem_names += E.qsems("qY", 4)
    sem_names += E.qsems("qW", 6)
    sem_names += E.qsems("qS", 4)
    sem_names += E.qsems("qO", 4)

    es = contextlib.ExitStack()
    with es:
        sb = lambda name, shape, dt: es.enter_context(nc.sbuf_tensor(name, shape, dt))
        xres = sb("xres", [128, NCH, D], F32)
        a1 = sb("a1", [128, DC * T], BF16)
        a2 = sb("a2", [128, NCH * D], BF16)
        wqkv = [sb(f"wqkv{j}", [128, 3 * DC, HD], BF16) for j in range(2)]
        hbuf = sb("hbuf", [128, 10240], BF16)
        lnw = sb("lnw", [128, 2, D], F32)
        idbf = sb("idbf", [128, 128], BF16)
        idf32 = sb("idf32", [128, 128], F32)
        rmat = sb("rmats", [128, 128], BF16)
        mw = sb("mws", [128, H, 4], F32)
        cpool = sb("cpools", [128, 12], F32)
        sigma = sb("sigma", [128, HD], F32)
        sbf = sb("sbf", [128, HD], BF16)
        sinb = sb("sinb", [128, HD], BF16)
        mg = sb("mg", [128, 4, HD], F32)
        stat = sb("stat", [128, 40], F32)
        gnst = sb("gnst", [128, 6, NCH * H], F32)
        bpp = sb("bpp", [128, 2, FC], F32)
        pA = es.enter_context(nc.psum_tensor("pA", [128, 512], F32))
        pB = es.enter_context(nc.psum_tensor("pB", [128, 512], F32))
        pC = es.enter_context(nc.psum_tensor("pC", [128, 1024], F32))
        pD = es.enter_context(nc.psum_tensor("pD", [128, 1024], F32))
        pb = es.enter_context(nc.psum_tensor("pb", [128, 1024], BF16))
        pb2 = es.enter_context(nc.psum_tensor("pb2", [128, 1024], BF16))

        sems = {s: es.enter_context(nc.semaphore(s)) for s in sem_names}

        qp = hbuf[:, 0:2048]
        kp = hbuf[:, 2048:4096]
        kd3 = hbuf[:, 4096:6144].rearrange("p (c k) -> p c k", k=128)
        vt3 = hbuf[:, 6144:8192].rearrange("p (c k) -> p c k", k=128)
        tst = hbuf[:, 8192:10240].rearrange("p (a b) -> p a b", b=512)
        wgq = hbuf[:, 2048:6144].rearrange("p (d c) -> p d c", d=DC)
        woh = hbuf[:, 6144:10240].rearrange("p (d c) -> p d c", d=DC)
        gtmp = hbuf[:, 0:2048].rearrange("p (a b) -> p a b", b=512)
        w2g = hbuf[:, 0:8192].rearrange("p (f d) -> p f d", f=8)
        wst = hbuf[:, 8192:10240].rearrange("p (a d k) -> p a d k", a=2, d=DC)

        st_mean = stat[:, 0:16]
        st_rstd = stat[:, 16:32]
        st_t1 = stat[:, 32:33]
        st_mx = stat[:, 33:34]

        a1t = lambda dc, t0, t1: a1[:, dc * T + t0: dc * T + t1]
        a2c = lambda c: a2[:, c * D:(c + 1) * D]
        a2ch = lambda c, h: a2[:, c * D + h * HD: c * D + (h + 1) * HD]
        h1sl = lambda fcg, t0, t1: a2[:, fcg * T + t0: fcg * T + t1]

        def dmay(fn, waits=()):
            return E.dma("y", "qY", fn, waits)

        def dmag(fn, waits=()):
            return E.dma("g", "qW", fn, waits)

        def dmas(fn, waits=()):
            return E.dma("s", "qS", fn, waits)

        def dmao(fn, waits=()):
            return E.dma("s", "qO", fn, waits)

        ld_x0 = dmay(lambda y: y.dma_start(
            out=xres[:], in_=x0_d.rearrange("c p d -> p c d")))

        es2 = contextlib.ExitStack()
        es.callback(es2.close)
        tabs = es2.enter_context(nc.sbuf_tensor("tabss", [128, 4, NCH, 128], BF16))
        caus = es2.enter_context(nc.sbuf_tensor("causs", [128, 128], F32))
        gq = es2.enter_context(nc.sbuf_tensor("gqs", [128, H, 128], BF16))
        gk = es2.enter_context(nc.sbuf_tensor("gks", [128, H, 128], BF16))
        t1 = dmag(lambda g: g.dma_start(
            out=tabs[:], in_=tabs_d.rearrange("t p c k -> p t c k")))
        t2 = dmag(lambda g: g.dma_start(out=gq[:], in_=gq_d))
        t3 = dmag(lambda g: g.dma_start(out=gk[:], in_=gk_d))
        for ap_, dst in [(caus_d, caus), (id_d, idbf),
                         (idf_d, idf32), (rm_d, rmat), (mw_d, mw), (cpool_d, cpool)]:
            dmay(lambda y, a=ap_, d=dst: y.dma_start(out=d[:], in_=a))
        ld_consts = E.qfull("qY") + [t1, t2, t3]

        def ln_stats(src3, wait_v=(), wait_s=()):
            for c in range(NCH):
                bank = pC if c % 2 == 0 else pD
                E.v(lambda v, c=c: v.reduce_sum(
                    st_mean[:, c:c + 1], src3[:, c, :], axis=AX.X), waits=wait_v)
                E.v(lambda v, c=c, b=bank: v.tensor_mul(
                    b[:], src3[:, c, :], src3[:, c, :]))
                E.v(lambda v, c=c, b=bank: v.reduce_sum(
                    st_rstd[:, c:c + 1], b[:], axis=AX.X))
            v_sum = E.c["v"]
            E.v(lambda v: v.tensor_scalar(
                st_mean[:], st_mean[:], 1.0 / D, None, ALU.mult),
                waits=[("v", v_sum)])
            E.v(lambda v: v.tensor_scalar(
                st_rstd[:], st_rstd[:], 1.0 / D, None, ALU.mult))
            E.v(lambda v: v.tensor_mul(gnst[:, 4, 0:16], st_mean[:], st_mean[:]))
            E.v(lambda v: v.tensor_sub(st_rstd[:], st_rstd[:], gnst[:, 4, 0:16]))
            v_fix = E.c["v"]
            s_std = E.s(lambda s: s.activation(
                st_rstd[:], st_rstd[:], AF.Sqrt, bias=cpool[:, 0:1]),
                waits=[("v", v_fix)] + ld_consts)
            v_r = E.v(lambda v: v.reciprocal(st_rstd[:], st_rstd[:]),
                      waits=[("s", s_std)])
            return v_r

        xres3 = xres

        def transpose_block_to_a1(src_c_ap_fn, wait_v_fn, dst_col_of_c, nblk=DC,
                                  extra_t_waits=()):
            evs = []
            for c in range(NCH):
                bank = pb if c % 2 == 0 else pb2
                wv = wait_v_fn(c)
                tws = list(extra_t_waits) + ([("v", wv)] if wv else [])
                if len(evs) >= 2:
                    tws.append(("v", evs[-2]))
                for j in range(nblk):
                    tt = E.t(lambda t, c=c, j=j, b=bank: t.transpose(
                        b[:, j * 128:(j + 1) * 128], src_c_ap_fn(c, j), idbf[:]),
                        waits=tws if j == 0 else ())
                t_done = E.c["t"]
                ev = E.v(lambda v, c=c, b=bank: v.tensor_copy(
                    a1[:].rearrange("p (d t) -> p d t", d=DC)[:, :, c * 128:(c + 1) * 128],
                    b[:].rearrange("p (d t) -> p d t", d=DC)),
                    waits=[("t", t_done)])
                evs.append(ev)
            return E.c["v"]

        if stage == "echo":
            for c in range(NCH):
                dmao(lambda s, c=c: s.dma_start(
                    out=y_d[c], in_=xres[:, c, :]), waits=[ld_x0])
            E.op("s", None, waits=E.qfull("qO"))

        if stage == "stats":
            for c in range(NCH):
                bank = pC if c % 2 == 0 else pD
                E.v(lambda v, c=c: v.reduce_sum(
                    st_mean[:, c:c + 1], xres3[:, c, :], axis=AX.X),
                    waits=[ld_x0])
                E.v(lambda v, c=c, b=bank: v.tensor_mul(
                    b[:], xres3[:, c, :], xres3[:, c, :]))
                E.v(lambda v, c=c, b=bank: v.reduce_sum(
                    st_rstd[:, c:c + 1], b[:], axis=AX.X))
            dmao(lambda s: s.dma_start(out=y_d[0][:, 0:40], in_=stat[:]),
                 waits=[("v", E.c["v"])])
            E.v(lambda v: v.tensor_scalar(
                st_mean[:], st_mean[:], 1.0 / D, None, ALU.mult),
                waits=E.qfull("qO"))
            E.v(lambda v: v.tensor_scalar(
                st_rstd[:], st_rstd[:], 1.0 / D, None, ALU.mult))
            E.v(lambda v: v.tensor_mul(gnst[:, 4, 0:16], st_mean[:], st_mean[:]))
            E.v(lambda v: v.tensor_sub(st_rstd[:], st_rstd[:], gnst[:, 4, 0:16]))
            dmao(lambda s: s.dma_start(out=y_d[1][:, 0:40], in_=stat[:]),
                 waits=[("v", E.c["v"])])
            s_std = E.s(lambda s: s.activation(
                st_rstd[:], st_rstd[:], AF.Sqrt, bias=cpool[:, 0:1]),
                waits=[("v", E.c["v"])] + ld_consts + E.qfull("qO"))
            E.v(lambda v: v.reciprocal(st_rstd[:], st_rstd[:]),
                waits=[("s", s_std)])
            dmao(lambda s: s.dma_start(out=y_d[2][:, 0:40], in_=stat[:]),
                 waits=[("v", E.c["v"])])
            E.op("s", None, waits=E.qfull("qO"))
        if stage not in ("echo", "stats"):
            vr = ln_stats(xres3, wait_v=[ld_x0], wait_s=[ld_x0])
            for c in range(NCH):
                E.v(lambda v, c=c: v.tensor_scalar(
                    xres3[:, c, :], xres3[:, c, :],
                    st_mean[:, c:c + 1], st_rstd[:, c:c + 1],
                    ALU.subtract, ALU.mult), waits=[("v", vr)] if c == 0 else ())
        v_m0 = E.c["v"]

        def tap_f32(src3, wv):
            for c in range(NCH):
                dmao(lambda s, c=c: s.dma_start(
                    out=y_d[c], in_=src3[:, c, :]),
                    waits=[("v", wv), ("s", E.c["s"])])
            E.op("s", None, waits=E.qfull("qO"))

        def tap_bf16(flat_src, wv, ws):
            for c in range(NCH):
                E.dma("g", "qO", lambda g, c=c: g.dma_start(
                    out=y_d[c], in_=flat_src[:, c * D:(c + 1) * D]),
                    waits=[("v", wv), ("s", ws)])
            E.op("g", None, waits=E.qfull("qO"))

        if stage == "m0":
            tap_f32(xres3, v_m0)

        def ln_to_a1(i, wkey, bkey, v_src_ready, s_src_ready, t_free_a1):
            ld0 = dmay(lambda y: y.dma_start(out=lnw[:, 0, :], in_=lnw_d[(i, wkey)]),
                       waits=[("v", v_src_ready)])
            ld1 = dmay(lambda y: y.dma_start(out=lnw[:, 1, :], in_=lnw_d[(i, bkey)]),
                       waits=[("v", v_src_ready)])
            vr = ln_stats(xres3, wait_v=[("v", v_src_ready)],
                          wait_s=[("s", s_src_ready), ("v", v_src_ready)])
            for c in range(NCH):
                E.v(lambda v, c=c: v.tensor_scalar(
                    a2[:, c * D:(c + 1) * D], xres3[:, c, :],
                    st_mean[:, c:c + 1], st_rstd[:, c:c + 1],
                    ALU.subtract, ALU.mult), waits=[("v", vr)] if c == 0 else ())
            a2v = a2[:].rearrange("p (c d) -> p c d", c=NCH)
            lnw0b = lnw[:, 0:1, :].broadcast_to([128, NCH, D])
            lnw1b = lnw[:, 1:2, :].broadcast_to([128, NCH, D])
            E.v(lambda v: v.tensor_tensor(a2v, a2v, lnw0b, ALU.mult),
                waits=[ld0])
            v_stage = E.v(lambda v: v.tensor_tensor(a2v, a2v, lnw1b, ALU.add),
                          waits=[ld1])
            return transpose_block_to_a1(
                lambda c, j: a2[:, c * D + j * 128: c * D + (j + 1) * 128],
                lambda c: v_stage, None, extra_t_waits=[("t", t_free_a1)])

        def retention(i, v_a1_ready):
            woffs = {"q": (0, wq_d[i]), "k": (DC, wk_d[i]), "v": (2 * DC, wv_d[i])}

            def load_head(h):
                lds = {}
                for kk, (off, wd) in woffs.items():
                    lds[kk] = dmag(lambda g, wd=wd, h=h, off=off: g.dma_start(
                        out=wqkv[h % 2][:, off:off + DC, :], in_=wd[h]),
                        waits=[("t", E.c["t"])])
                return lds

            ldh = [load_head(0), load_head(1)]
            pCa, pCb = pC[:, 0:512], pC[:, 512:1024]
            pDa, pDb = pD[:, 0:128], pD[:, 512:640]
            yCa, yCb = pC[:, 0:128], pC[:, 512:640]

            for h in range(H):
                slot = wqkv[h % 2]
                w_ready = ldh[h]
                hd_v0 = E.c["v"]
                hd_s0 = E.c["s"]

                def proj(woff, wkey, ti_sin, ti_cos, dstbuf, copy_waits=()):
                    vts = {}
                    rot_s = {}
                    rot_t = {}
                    for tb in range(4):
                        bank = pA if tb % 2 == 0 else pB
                        rbank = pCa if tb % 2 == 0 else pCb
                        tw = [w_ready[wkey], ("v", v_a1_ready)]
                        if tb >= 2:
                            tw.append(("v", vts[tb - 2]))
                        else:
                            tw += [("v", hd_v0), ("s", hd_s0)]
                        for dc in range(DC):
                            E.t(lambda t, dc=dc, tb=tb, bank=bank, slot=slot,
                                woff=woff: t.matmul(
                                bank[:], slot[:, woff + dc, :],
                                a1t(dc, tb * 512, (tb + 1) * 512),
                                start=(dc == 0), stop=(dc == DC - 1)),
                                waits=tw if dc == 0 else ())
                        tmm = E.c["t"]
                        sl_s = tst[:, 2 * (tb % 2), :].rearrange("p (a b) -> p a b", b=128)
                        sl_c = tst[:, 2 * (tb % 2) + 1, :].rearrange("p (a b) -> p a b", b=128)
                        bank3 = bank[:].rearrange("p (a b) -> p a b", b=128)
                        vw = [("t", tmm)] + ld_consts
                        if tb >= 2:
                            vw.append(("t", rot_t[tb - 2]))
                        E.v(lambda v, b3=bank3, sl=sl_s, ti=ti_sin, tb=tb:
                            v.tensor_tensor(
                                sl, b3, tabs[:, ti, 4 * tb:4 * tb + 4, :], ALU.mult),
                            waits=vw)
                        vt = E.v(lambda v, b3=bank3, sl=sl_c, ti=ti_cos, tb=tb:
                                 v.tensor_tensor(
                                     sl, b3, tabs[:, ti, 4 * tb:4 * tb + 4, :], ALU.mult))
                        vts[tb] = vt
                        tw2 = [("v", vt)]
                        if tb >= 2:
                            tw2.append(("s", rot_s[tb - 2]))
                        E.t(lambda t, rb=rbank, tb=tb: t.matmul(
                            rb, rmat[:], tst[:, 2 * (tb % 2), :],
                            start=True, stop=False), waits=tw2)
                        E.t(lambda t, rb=rbank, tb=tb: t.matmul(
                            rb, idbf[:], tst[:, 2 * (tb % 2) + 1, :],
                            start=False, stop=True))
                        rot_t[tb] = E.c["t"]
                        rot_s[tb] = E.s(lambda s, rb=rbank, tb=tb, dst=dstbuf: s.copy(
                            dst[:, tb * 512:(tb + 1) * 512], rb),
                            waits=[("t", rot_t[tb])] +
                                  (list(copy_waits) if tb == 0 else []))
                    return rot_s[3]

                s_q = proj(0, "q", 0, 1, qp, copy_waits=[sp_guard[0]])
                qp3 = qp.rearrange("p (c k) -> p c k", k=128)
                v_qg = E.v(lambda v, h=h: v.tensor_tensor(
                    qp3, qp3, gq[:, h:h + 1, :].broadcast_to([128, NCH, 128]),
                    ALU.mult), waits=[("s", s_q)])
                sp_guard[0] = dmag(lambda g, h=h: g.dma_start(
                    out=qsp_d[i, h], in_=qp), waits=[("v", v_qg)])
                s_k = proj(DC, "k", 2, 3, kp)
                kp3 = kp.rearrange("p (c k) -> p c k", k=128)
                v_kg = E.v(lambda v, h=h: v.tensor_tensor(
                    kp3, kp3, gk[:, h:h + 1, :].broadcast_to([128, NCH, 128]),
                    ALU.mult), waits=[("s", s_k)])
                s_kd_half = {}
                for half in range(2):
                    bank = pb if half == 0 else pb2
                    for j in range(8):
                        E.t(lambda t, half=half, j=j, b=bank: t.transpose(
                            b[:, j * 128:(j + 1) * 128],
                            kp[:, (half * 8 + j) * 128:(half * 8 + j + 1) * 128],
                            idbf[:]),
                            waits=[("v", v_kg)] if j == 0 else ())
                    tkd = E.c["t"]
                    s_kd_half[half] = E.s(lambda s, half=half, b=bank: s.copy(
                        kd3[:, half * 8: half * 8 + 8, :],
                        b[:].rearrange("p (a k) -> p a k", k=128)),
                        waits=[("t", tkd)])
                s_kd = s_kd_half[1]
                s_vst = {}
                t_vtr = {}
                for tb in range(4):
                    bank = pA if tb % 2 == 0 else pB
                    tw = [w_ready["v"]]
                    if tb >= 2:
                        tw.append(("s", s_vst[tb - 2]))
                    else:
                        tw += [("v", E.c["v"]), ("s", E.c["s"])]
                    for dc in range(DC):
                        E.t(lambda t, dc=dc, bank=bank, tb=tb, slot=slot: t.matmul(
                            bank[:], slot[:, 2 * DC + dc, :],
                            a1t(dc, tb * 512, (tb + 1) * 512),
                            start=(dc == 0), stop=(dc == DC - 1)),
                            waits=tw if dc == 0 else ())
                    tmm = E.c["t"]
                    sw = [("t", tmm)]
                    if tb >= 2:
                        sw.append(("t", t_vtr[tb - 2]))
                    s_vst[tb] = E.s(lambda s, bank=bank, tb=tb: s.copy(
                        tst[:, tb % 2, :], bank[:]), waits=sw)
                    bank2 = pb if tb % 2 == 0 else pb2
                    tvw = [("s", s_vst[tb])]
                    if tb >= 2:
                        tvw.append(("s", s_vt3[tb - 2]))
                    for j in range(4):
                        E.t(lambda t, tb=tb, j=j, b2=bank2: t.transpose(
                            b2[:, j * 128:(j + 1) * 128],
                            tst[:, tb % 2, j * 128:(j + 1) * 128], idbf[:]),
                            waits=tvw if j == 0 else ())
                    t_vtr[tb] = E.c["t"]
                    s_vt3[tb] = E.s(lambda s, tb=tb, b2=bank2: s.copy(
                        vt3[:, tb * 4:tb * 4 + 4, :],
                        b2[:, 0:512].rearrange("p (a k) -> p a k", k=128)),
                        waits=[("t", t_vtr[tb])])
                s_v = E.c["s"]
                t_projend = E.c["t"]
                if h + 2 < H:
                    ldh.append(load_head(h + 2))
                v_ms = E.v(lambda v: v.memzero(sigma[:]),
                           waits=[exp_guard[0]])
                s_ms = E.s(lambda s: s.memzero(sbf[:]), waits=[("v", v_ms)])
                vsc, yev, v_sig, s_sbf, t_ylast, t_ds = {}, {}, {}, {}, {}, {}

                def emit_scores_ds(c, h=h):
                    sbank = pA if c % 2 == 0 else pB
                    dbank = pDa if c % 2 == 0 else pDb
                    scs = sc_slots[c % 2]
                    tw = [("v", v_kg), ("s", s_v)]
                    if c >= 2:
                        tw.append(("v", vsc[c - 2]))
                    E.t(lambda t, c=c, b=sbank: t.matmul(
                        b[:, 0:128], kp[:, c * 128:(c + 1) * 128],
                        qp[:, c * 128:(c + 1) * 128], start=True, stop=True),
                        waits=tw)
                    t_sc = E.c["t"]
                    vsc[c] = E.v(lambda v, c=c, b=sbank, scs=scs: v.tensor_tensor(
                        scs, b[:, 0:128], caus[:], ALU.mult),
                        waits=[("t", t_sc)])
                    tw3 = [("s", s_kd)]
                    if c >= 2:
                        tw3.append(("v", v_sig[c - 2]))
                    E.t(lambda t, c=c, db=dbank: t.matmul(
                        db, kd3[:, c, :], vt3[:, c, :], start=True, stop=True),
                        waits=tw3)
                    t_ds[c] = E.c["t"]

                def emit_sig(c, h=h):
                    dbank = pDa if c % 2 == 0 else pDb
                    vw2 = [("t", t_ds[c])]
                    if c > 0:
                        vw2.append(("s", s_sbf[c - 1]))
                    v_sig[c] = E.v(lambda v, c=c, db=dbank, h=h:
                                   v.scalar_tensor_tensor(
                                       sigma[:], sigma[:], g128[h], db,
                                       ALU.mult, ALU.add), waits=vw2)

                def emit_sbf(c, h=h):
                    sw = [("v", v_sig[c]), ("t", t_ylast[c])]
                    s_sbf[c] = E.s(lambda s, c=c, h=h: s.activation(
                        sbf[:], sigma[:], AF.Copy, scale=cpool[:, 1 + h:2 + h]),
                        waits=sw)

                def emit_y(c, h=h):
                    ybank = yCa if c % 2 == 0 else yCb
                    scs = sc_slots[c % 2]
                    tw2 = [("v", vsc[c])]
                    if c >= 2:
                        tw2.append(("s", yev[c - 2]))
                    E.t(lambda t, c=c, yb=ybank, scs=scs: t.matmul(
                        yb, scs, vt3[:, c, :],
                        start=True, stop=(c == 0)), waits=tw2)
                    if c > 0:
                        E.t(lambda t, c=c, yb=ybank: t.matmul(
                            yb, qp[:, c * 128:(c + 1) * 128], sbf[:],
                            start=False, stop=True),
                            waits=[("s", s_sbf[c - 1])])
                    t_ylast[c] = E.c["t"]
                    yev[c] = E.s(lambda s, c=c, yb=ybank, h=h: s.copy(
                        a2ch(c, h), yb), waits=[("t", t_ylast[c])])

                emit_scores_ds(0)
                emit_scores_ds(1)
                emit_sig(0)
                for c in range(NCH):
                    emit_y(c)
                    if c + 2 < NCH:
                        emit_scores_ds(c + 2)
                    emit_sbf(c)
                    if c + 1 < NCH:
                        emit_sig(c + 1)
                exp_guard[0] = dmag(lambda g, h=h: g.dma_start(
                    out=cc_in[i].ap()[h], in_=sigma[:]),
                    waits=[("v", v_sig[NCH - 1])])
                import os as _os
                if h == 0 and _os.environ.get("KDBG_EARLYDUMP") and stage == "dbgret":
                    E.dma("g", "qO", lambda g: g.dma_start(
                        out=dbg_d["dbgy"].rearrange("p (c k) -> p c k", k=128),
                        in_=a2[:].rearrange("p (c d) -> p c d", c=NCH)[:, :, 0:128]),
                        waits=[("s", E.c["s"])])
            E.op("g", lambda g: g.collective_compute(
                "AllGather", ALU.bypass,
                replica_groups=[[0, 1, 2, 3], [4, 5, 6, 7]],
                ins=[cc_in[i].ap()], outs=[cc_out[i].ap()]),
                waits=E.qfull("qW"), inc="cc", n=1)
            return E.c["cc"], E.c["t"], E.c["s"], E.c["v"]

        def corrections(i, cc_n, t_chain, s_chain):
            vcor = {}
            for h in range(H):
                rel = dmag(lambda g, h=h: g.dma_start(out=qp, in_=qsp_d[i, h]),
                           waits=[("t", E.c["t"])])
                ldm = dmag(lambda g, h=h: g.dma_start(
                    out=mg[:],
                    in_=cc_out[i].ap().rearrange("(r h) p k -> r h p k", h=H)
                        [:, h].rearrange("r p k -> p r k")),
                    waits=[("cc", cc_n)])
                E.v(lambda v, h=h: v.tensor_scalar(
                    sigma[:], mg[:, 0, :], mw[:, h, 0:1], None, ALU.mult),
                    waits=[ldm, ("s", s_chain), exp_guard[0]] + ld_consts)
                for r in range(1, 4):
                    E.v(lambda v, h=h, r=r: v.scalar_tensor_tensor(
                        sigma[:], mg[:, r, :], mw[:, h, r:r + 1], sigma[:],
                        ALU.mult, ALU.add))
                v_si = E.v(lambda v: v.tensor_copy(sinb[:], sigma[:]))
                for c in range(NCH):
                    bank = pA if c % 2 == 0 else pB
                    tw = [("v", v_si), rel]
                    if c >= 2:
                        tw.append(("v", vcor[c - 2]))
                    E.t(lambda t, c=c, b=bank: t.matmul(
                        b[:, 0:128], qp[:, c * 128:(c + 1) * 128], sinb[:],
                        start=True, stop=True), waits=tw)
                    t_c = E.c["t"]
                    vcor[c] = E.v(lambda v, c=c, h=h, b=bank:
                                  v.scalar_tensor_tensor(
                                      a2ch(c, h), b[:, 0:128],
                                      float(gam[h] ** (128 * c)),
                                      a2ch(c, h), ALU.mult, ALU.add),
                                  waits=[("t", t_c)])
            return E.c["v"]

        def gn_and_z(i, v_yfinal):
            ld0 = dmay(lambda y: y.dma_start(out=lnw[:, 0, :], in_=lnw_d[(i, "gnw")]),
                       waits=[("v", v_yfinal)])
            ld1 = dmay(lambda y: y.dma_start(out=lnw[:, 1, :], in_=lnw_d[(i, "gnb")]),
                       waits=[("v", v_yfinal)])
            vgn = {}
            for c in range(NCH):
                bank = pC if c % 2 == 0 else pD
                E.v(lambda v, c=c: v.reduce_sum(
                    gnst[:, 0, c * 8:(c + 1) * 8],
                    a2c(c).rearrange("p (h k) -> p h k", h=H), axis=AX.X),
                    waits=[("v", v_yfinal)] if c == 0 else ())
                vw = []
                if c >= 2:
                    vw.append(("v", vgn[c - 2]))
                E.v(lambda v, c=c, b=bank: v.tensor_mul(
                    b[:], a2c(c), a2c(c)), waits=vw)
                vgn[c] = E.v(lambda v, c=c, b=bank: v.reduce_sum(
                    gnst[:, 1, c * 8:(c + 1) * 8],
                    b[:].rearrange("p (h k) -> p h k", h=H), axis=AX.X))
            E.v(lambda v: v.tensor_scalar(
                gnst[:, 4, :], gnst[:, 0, :], 1.0 / HD, None, ALU.mult))
            E.v(lambda v: v.tensor_scalar(
                gnst[:, 2, :], gnst[:, 1, :], 1.0 / HD, None, ALU.mult))
            E.v(lambda v: v.tensor_mul(gnst[:, 5, :], gnst[:, 4, :], gnst[:, 4, :]))
            E.v(lambda v: v.tensor_sub(gnst[:, 2, :], gnst[:, 2, :], gnst[:, 5, :]))
            v_var = E.c["v"]
            s_std = E.s(lambda s: s.activation(
                gnst[:, 2, :], gnst[:, 2, :], AF.Sqrt, bias=cpool[:, 0:1]),
                waits=[("v", v_var)])
            E.v(lambda v: v.reciprocal(gnst[:, 2, :], gnst[:, 2, :]),
                waits=[("s", s_std)])
            v_c0 = E.v(lambda v: v.scalar_tensor_tensor(
                gnst[:, 3, :], gnst[:, 4, :], -1.0, gnst[:, 2, :],
                ALU.mult, ALU.mult))
            wg_full = hbuf[:, 0:8192].rearrange("p (d x) -> p d x", d=DC)
            ld_wg = dmag(lambda g: g.dma_start(out=wg_full, in_=wg_d[i]),
                         waits=[("t", E.c["t"]), ("v", v_yfinal)])
            gt0 = hbuf[:, 8192:8704]
            gt1 = hbuf[:, 8704:9216]
            szT = {}
            tgz = {}
            for c in range(NCH):
                for h in range(H):
                    E.v(lambda v, c=c, h=h: v.tensor_scalar(
                        a2ch(c, h), a2ch(c, h),
                        gnst[:, 2, c * 8 + h:c * 8 + h + 1],
                        gnst[:, 3, c * 8 + h:c * 8 + h + 1],
                        ALU.mult, ALU.add),
                        waits=[("v", v_c0)] if (c == 0 and h == 0) else ())
                E.v(lambda v, c=c: v.tensor_tensor(
                    a2c(c), a2c(c), lnw[:, 0, :], ALU.mult),
                    waits=[ld0] if c == 0 else ())
                v_yn = E.v(lambda v, c=c: v.tensor_tensor(
                    a2c(c), a2c(c), lnw[:, 1, :], ALU.add),
                    waits=[ld1] if c == 0 else ())
                for half in range(2):
                    bank = pCa2 if half == 0 else pDa2
                    gts = gt0 if half == 0 else gt1
                    tw = [ld_wg, ("v", v_c0)]
                    if c >= 1:
                        tw.append(("v", vz_done[0]))
                    for dc in range(DC):
                        E.t(lambda t, c=c, dc=dc, half=half, b=bank: t.matmul(
                            b, a1t(dc, c * 128, (c + 1) * 128),
                            wg_full[:, dc, half * 512:(half + 1) * 512],
                            start=(dc == 0), stop=(dc == DC - 1)),
                            waits=tw if dc == 0 else ())
                    t_g = E.c["t"]
                    sg = E.s(lambda s, b=bank, gts=gts: s.activation(
                        gts, b, AF.Silu), waits=[("t", t_g)])
                    vz_done[0] = E.v(lambda v, c=c, half=half, gts=gts:
                                     v.tensor_tensor(
                                         a2[:, c * D + half * 512:
                                            c * D + (half + 1) * 512],
                                         a2[:, c * D + half * 512:
                                            c * D + (half + 1) * 512],
                                         gts, ALU.mult),
                                     waits=[("s", sg), ("v", v_yn)])
                v_z = E.c["v"]
                bank2 = pb if c % 2 == 0 else pb2
                tzw = [("v", v_z)]
                if c >= 2:
                    tzw.append(("s", szT[c - 2]))
                for j in range(DC):
                    E.t(lambda t, c=c, j=j, b2=bank2: t.transpose(
                        b2[:, j * 128:(j + 1) * 128],
                        a2[:, c * D + j * 128:c * D + (j + 1) * 128], idbf[:]),
                        waits=tzw if j == 0 else ())
                tgz[c] = E.c["t"]
                szT[c] = E.s(lambda s, c=c, b2=bank2: s.copy(
                    a1[:].rearrange("p (d t) -> p d t", d=DC)
                    [:, :, c * 128:(c + 1) * 128],
                    b2[:].rearrange("p (d t) -> p d t", d=DC)),
                    waits=[("t", tgz[c])])
            return E.c["s"], E.c["t"]

        def wo_residual(i, s_zT, t_gz):
            wo_full = hbuf[:, 0:8192].rearrange("p (d x) -> p d x", d=DC)
            ld = dmag(lambda g: g.dma_start(out=wo_full, in_=wo_d[i]),
                      waits=[("t", t_gz)])
            vwo = {}
            for c in range(NCH):
                bank = pC if c % 2 == 0 else pD
                tw = [ld, ("s", s_zT)]
                if c >= 2:
                    tw.append(("v", vwo[c - 2]))
                for dc in range(DC):
                    for dh in range(2):
                        E.t(lambda t, c=c, dc=dc, dh=dh, b=bank: t.matmul(
                            b[:, dh * 512:(dh + 1) * 512],
                            a1t(dc, c * 128, (c + 1) * 128),
                            wo_full[:, dc, dh * 512:(dh + 1) * 512],
                            start=(dc == 0), stop=(dc == DC - 1)),
                            waits=tw if (dc == 0 and dh == 0) else ())
                t_w = E.c["t"]
                vwo[c] = E.v(lambda v, c=c, b=bank: v.tensor_tensor(
                    xres3[:, c, :], b[:], xres3[:, c, :], ALU.add),
                    waits=[("t", t_w)])
            return E.c["v"], E.c["t"]

        def ffn(i, v_a1_ready, strips_d, groups_d, bpp_slot, t_free_hbuf,
                lora=False, init_copy=False, bias_dram=None):
            w2gv = hbuf[:, 0:8192].rearrange("p (f d) -> p f d", f=8)
            wstv = hbuf[:, 8192:10240].rearrange("p (a d k) -> p a d k", a=2, d=DC)
            ldb = dmay(lambda y: y.dma_start(
                out=bpp[:, bpp_slot, :],
                in_=lnw_d[(i, "b1pp")] if bias_dram is None else bias_dram))
            sev = {}
            vff = {}
            lds = {}
            act = AF.Relu if lora else AF.Silu
            t_guard = t_free_hbuf
            for g in range(4):
                ldg = dmag(lambda g_, g=g: g_.dma_start(
                    out=w2gv, in_=groups_d[g]),
                    waits=[("t", max(t_guard, E.c["t"] if g > 0 else 0))])
                if lora:
                    ld_bcat[0] = dmag(lambda g_, g=g: g_.dma_start(
                        out=bcat_sb[0:32, g % 2, :],
                        in_=bcat_d[0:32, g * 1024:(g + 1) * 1024]),
                        waits=[("t", E.c["t"])])
                for fcg in range(8):
                    fc = g * 8 + fcg
                    lds[fc] = dmag(lambda g_, fc=fc: g_.dma_start(
                        out=wstv[:, fc % 2], in_=strips_d[fc]),
                        waits=[("t", E.c["t"])])
                    for tb in range(4):
                        bank = pA if tb % 2 == 0 else pB
                        tw = [lds[fc], ("v", v_a1_ready)]
                        key = fc * 4 + tb
                        if key >= 2:
                            tw.append(("s", sev[key - 2]))
                        for dc in range(DC):
                            E.t(lambda t, dc=dc, fc=fc, tb=tb, b=bank: t.matmul(
                                b[:], wstv[:, fc % 2, dc, :],
                                a1t(dc, tb * 512, (tb + 1) * 512),
                                start=(dc == 0),
                                stop=(dc == DC - 1 and not lora)),
                                waits=tw if dc == 0 else ())
                        if lora:
                            E.t(lambda t, fc=fc, tb=tb, b=bank, g=g,
                                fcg=fcg: t.matmul(
                                b[:], bcat_sb[0:32, g % 2,
                                              fcg * 128:(fcg + 1) * 128],
                                umt[0:32, tb * 512:(tb + 1) * 512],
                                start=False, stop=True),
                                waits=[("v", v_umt[0]), ld_bcat[0]])
                        t_m = E.c["t"]
                        sev[key] = E.s(
                            lambda s, fcg=fcg, tb=tb, b=bank, fc=fc:
                            s.activation(
                                h1sl(fcg, tb * 512, (tb + 1) * 512), b[:],
                                act, bias=bpp[:, bpp_slot, fc:fc + 1]),
                            waits=[("t", t_m), ldb])
                s_h1 = E.c["s"]
                for c in range(NCH):
                    bank = pC if c % 2 == 0 else pD
                    tw = [ldg, ("s", s_h1)]
                    if c >= 2:
                        tw.append(("v", vff[(g, c - 2)]))
                    elif g > 0:
                        tw.append(("v", vff[(g - 1, c + 14)]))
                    for fcg in range(8):
                        for dh in range(2):
                            E.t(lambda t, c=c, fcg=fcg, dh=dh, b=bank: t.matmul(
                                b[:, dh * 512:(dh + 1) * 512],
                                h1sl(fcg, c * 128, (c + 1) * 128),
                                w2gv[:, fcg, dh * 512:(dh + 1) * 512],
                                start=(fcg == 0), stop=(fcg == 7)),
                                waits=tw if (fcg == 0 and dh == 0) else ())
                    t_w2 = E.c["t"]
                    if init_copy and g == 0:
                        vff[(g, c)] = E.v(lambda v, c=c, b=bank: v.tensor_copy(
                            xres3[:, c, :], b[:]),
                            waits=[("t", t_w2), ("v", v_xres_free[0])] +
                                  x2sp_guard[0])
                    else:
                        vff[(g, c)] = E.v(lambda v, c=c, b=bank: v.tensor_tensor(
                            xres3[:, c, :], b[:], xres3[:, c, :], ALU.add),
                            waits=[("t", t_w2)])
                t_guard = 0
            return E.c["v"], E.c["t"]

        def add_bias_big(v_wait, dram_ap):
            ld = dmay(lambda y: y.dma_start(out=lnw[:, 0, :], in_=dram_ap),
                      waits=[("v", v_wait)])
            return E.v(lambda v: v.tensor_tensor(
                xres3[:, :, :], xres3[:, :, :],
                lnw[:, 0:1, :].broadcast_to([128, NCH, D]), ALU.add),
                waits=[ld, ("v", v_wait)])

        exp_guard = [("cc", 0)]
        sp_guard = [("cc", 0)]
        vz_done = [0]
        v_xres_free = [0]
        x2sp_guard = [0]
        sc_slots = None
        pCa2, pDa2 = pC[:, 0:512], pD[:, 0:512]
        s_vt3 = {}
        v_umt = [0]
        ld_bcat = [0]

        v_prev_v = v_m0
        if stage == "dbgret":
            sc_slots = [tst[:, 2, 0:128], tst[:, 2, 128:256]]
            v_a1 = ln_to_a1(0, "ln1w", "ln1b", v_prev_v, E.c["s"], E.c["t"])
            Hsave = globals()["H"]
            import os
            globals()["H"] = int(os.environ.get("KDBG_H", "1"))
            cc_n, t_ch, s_ch, v_ch = retention(0, v_a1)
            globals()["H"] = Hsave
            for nm, src_ap in (("dbgq", qp), ("dbgk", kp),
                               ("dbgv", hbuf[:, 6144:8192]),
                               ("dbgs", None)):
                if nm == "dbgs":
                    E.dma("g", "qO", lambda g: g.dma_start(
                        out=dbg_d["dbgs"][:, 0:128], in_=sigma[:]),
                        waits=[("v", E.c["v"]), ("s", E.c["s"]),
                               ("t", E.c["t"])])
                else:
                    E.dma("g", "qO", lambda g, nm=nm, sa=src_ap: g.dma_start(
                        out=dbg_d[nm], in_=sa),
                        waits=[("v", E.c["v"]), ("s", E.c["s"]),
                               ("t", E.c["t"])])
            import os as _os2
            if not _os2.environ.get("KDBG_EARLYDUMP"):
                E.dma("g", "qO", lambda g: g.dma_start(
                    out=dbg_d["dbgy"].rearrange("p (c k) -> p c k", k=128),
                    in_=a2[:].rearrange("p (c d) -> p c d", c=NCH)[:, :, 0:128]),
                    waits=[("s", E.c["s"])])
            E.dma("g", "qO", lambda g: g.dma_start(
                out=dbg_d["dbga"], in_=a2[:, 0:8192]),
                waits=[("s", E.c["s"])])
            E.dma("g", "qO", lambda g: g.dma_start(
                out=dbg_d["dbgb"], in_=a2[:, 8192:16384]),
                waits=[("s", E.c["s"])])
            E.op("g", None, waits=E.qfull("qO"))

        if stage_i >= _ORDER.index("yraw0"):
            sc_slots = [tst[:, 2, 0:128], tst[:, 2, 128:256]]
            for i in range(NL):
                s_vt3 = {}
                v_a1 = ln_to_a1(i, "ln1w", "ln1b", v_prev_v, E.c["s"], E.c["t"])
                cc_n, t_ch, s_ch, v_ch = retention(i, v_a1)
                v_yfin = corrections(i, cc_n, t_ch, s_ch)
                if stage == ("yraw0" if i == 0 else "yraw1"):
                    tap_bf16(a2, v_yfin, E.c["s"])
                    break
                s_zT, t_gz = gn_and_z(i, v_yfin)
                v_xr, t_wo = wo_residual(i, s_zT, t_gz)
                if stage == ("wo0" if i == 0 else "wo1"):
                    tap_f32(xres3, v_xr)
                    break
                v_a1b = ln_to_a1(i, "ln2w", "ln2b", v_xr, E.c["s"], E.c["t"])
                v_ff, t_ff = ffn(i, v_a1b, w1_d[i], w2_d[i], 0, t_wo)
                v_prev_v = add_bias_big(v_ff, lnw_d[(i, "b2bc")])
                if stage == ("ffn0" if i == 0 else "ffn1"):
                    tap_f32(xres3, v_prev_v)
                    break

        if stage_i >= _ORDER.index("x2"):
            es2.close()
            rbuf = sb("rbuf", [128, 2, D], F32)
            obuf = sb("obuf", [128, 2, D], F32)
            logits = sb("logits", [128, NCH, 8], F32)
            msk8 = sb("msk8", [128, NCH, 8], F32)
            m32 = sb("m32", [128, NCH, 8, 4], BF16)
            um = sb("um", [128, NCH, 32], BF16)
            umt = sb("umt", [32, T], BF16)
            acat_sb = sb("acatsb", [128, DC, 32], BF16)
            bcat_sb = sb("bcatsb", [32, 2, 1024], BF16)
            gw_sb = sb("gwsb", [128, DC, 8], F32)
            gb_sb = sb("gbsb", [128, 8], F32)
            x2t = sb("x2t", [128, DC, 128], F32)

            ld_small = []
            for ap_, dst in [(gw_d, gw_sb), (gateb_d, gb_sb)]:
                ld_small.append(dmay(lambda y, a=ap_, d=dst: y.dma_start(
                    out=d[:], in_=a), waits=[("v", E.c["v"]), ("t", E.c["t"])]))
            ld_acat = dmag(lambda g: g.dma_start(out=acat_sb[:], in_=acat_d),
                           waits=[("t", E.c["t"])])

            vr = ln_stats(xres3, wait_v=[("v", v_prev_v)],
                          wait_s=[("v", v_prev_v)])
            vx2 = {}
            for c in range(NCH):
                rl = dmas(lambda s, c=c: s.dma_start(
                    out=rbuf[:, c % 2, :], in_=x0_d[c]),
                    waits=([("v", vr)] if c < 2 else [("v", vx2[c - 2])]))
                E.v(lambda v, c=c: v.tensor_scalar(
                    obuf[:, c % 2, :], xres3[:, c, :],
                    st_mean[:, c:c + 1], st_rstd[:, c:c + 1],
                    ALU.subtract, ALU.mult),
                    waits=([("v", vr)] if c == 0 else []) +
                          ([("v", vx2[c - 2])] if c >= 2 else []))
                vx2[c] = E.v(lambda v, c=c: v.tensor_tensor(
                    xres3[:, c, :], obuf[:, c % 2, :], rbuf[:, c % 2, :],
                    ALU.add), waits=[rl])
                dmas(lambda s, c=c: s.dma_start(
                    out=x2_d[c], in_=xres3[:, c, :]), waits=[("v", vx2[c])])
            s_x2spill = E.qfull("qS")
            v_x2 = E.c["v"]
            if stage == "x2":
                tap_f32(xres3, v_x2)
            else:
                vr2 = ln_stats(xres3, wait_v=[("v", v_x2)],
                               wait_s=[("v", v_x2)])
                for c in range(NCH):
                    E.v(lambda v, c=c: v.tensor_scalar(
                        a2[:, c * D:(c + 1) * D], xres3[:, c, :],
                        st_mean[:, c:c + 1], st_rstd[:, c:c + 1],
                        ALU.subtract, ALU.mult),
                        waits=[("v", vr2)] if c == 0 else ())
                v_stage2 = E.c["v"]
                v_a1m = transpose_block_to_a1(
                    lambda c, j: a2[:, c * D + j * 128: c * D + (j + 1) * 128],
                    lambda c: v_stage2, None, extra_t_waits=[("t", E.c["t"])])
                vgt = {}
                sgt = {}
                for c in range(NCH):
                    E.v(lambda v, c=c: v.tensor_scalar(
                        obuf[:, c % 2, :], xres3[:, c, :],
                        st_mean[:, c:c + 1], st_rstd[:, c:c + 1],
                        ALU.subtract, ALU.mult),
                        waits=([("v", vr2)] if c == 0 else []) +
                              ([("v", vgt[c - 2])] if c >= 2 else []))
                    v_ob = E.c["v"]
                    tw = [("v", v_ob)]
                    if c >= 2:
                        tw.append(("v", vgt[c - 2]))
                    for j in range(DC):
                        E.t(lambda t, c=c, j=j: t.transpose(
                            pC[:, j * 128:(j + 1) * 128],
                            obuf[:, c % 2, j * 128:(j + 1) * 128], idf32[:]),
                            waits=tw if j == 0 else ())
                    t_tr = E.c["t"]
                    vgt[c] = E.v(lambda v, c=c: v.tensor_copy(
                        x2t[:].rearrange("p d k -> p (d k)"), pC[:]),
                        waits=[("t", t_tr)])
                    for dc in range(DC):
                        E.t(lambda t, c=c, dc=dc: t.matmul(
                            pD[:, c * 8: (c + 1) * 8],
                            x2t[:, dc, :], gw_sb[:, dc, :],
                            start=(dc == 0), stop=(dc == DC - 1)),
                            waits=[("v", vgt[c]), ld_small[0],
                                   ("t", E.c["t"])] if dc == 0 else ())
                    sgt[c] = E.c["t"]
                t_gate = E.c["t"]
                v_lg = E.v(lambda v: v.tensor_tensor(
                    logits[:, :, :],
                    pD[:, 0:128].rearrange("p (c e) -> p c e", e=8),
                    gb_sb[:, 0:8].rearrange("p (a e) -> p a e", a=1)
                    .broadcast_to([128, NCH, 8]), ALU.add),
                    waits=[("t", t_gate), ld_small[1]])
                for c in range(NCH):
                    E.v(lambda v, c=c: v.reduce_max(
                        st_mx[:], logits[:, c, :], axis=AX.X),
                        waits=[("v", v_lg)] if c == 0 else ())
                    E.v(lambda v, c=c: v.tensor_scalar(
                        msk8[:, c, :], logits[:, c, :], st_mx[:], None,
                        ALU.is_equal))
                v_msk = E.c["v"]
                for r in range(4):
                    E.v(lambda v, r=r: v.tensor_copy(
                        m32[:, :, :, r], msk8[:]), waits=[("v", v_msk)])
                v_m32 = E.c["v"]
                sumt = {}
                for c in range(NCH):
                    tw = [("v", v_a1m), ld_acat]
                    if c >= 2:
                        tw.append(("v", sumt[c - 2][0]))
                    for dc in range(DC):
                        E.t(lambda t, c=c, dc=dc: t.matmul(
                            pA[:, 0:32], a1t(dc, c * 128, (c + 1) * 128),
                            acat_sb[:, dc, :],
                            start=(dc == 0), stop=(dc == DC - 1)),
                            waits=tw if dc == 0 else ())
                    t_u = E.c["t"]
                    vu = E.v(lambda v, c=c: v.tensor_tensor(
                        um[:, c, :], pA[:, 0:32],
                        m32[:, c, :, :].rearrange("p e r -> p (e r)"),
                        ALU.mult), waits=[("t", t_u), ("v", v_m32)])
                    bank2 = pb if c % 2 == 0 else pb2
                    tw2 = [("v", vu)]
                    if c >= 2:
                        tw2.append(("s", sumt[c - 2][1]))
                    E.t(lambda t, c=c, b2=bank2: t.transpose(
                        b2[0:32, 0:128], um[:, c, :], idbf[:]),
                        waits=tw2)
                    t_ut = E.c["t"]
                    s_ut = E.s(lambda s, c=c, b2=bank2: s.copy(
                        umt[0:32, c * 128:(c + 1) * 128], b2[0:32, 0:128]),
                        waits=[("t", t_ut)])
                    sumt[c] = (vu, s_ut)
                v_umt[0] = E.c["v"]
                s_umt = E.c["s"]
                v_xres_free[0] = E.c["v"]
                x2sp_guard[0] = s_x2spill
                v_F, t_F = ffn(None, v_a1m, wi_d, wom_d, 1, E.c["t"],
                               lora=True, init_copy=True, bias_dram=wib_d)
                v_F = add_bias_big(v_F, wob_d)
                vr3 = ln_stats(xres3, wait_v=[("v", v_F)], wait_s=[("v", v_F)])
                vfin = {}
                qo_c = {}
                for c in range(NCH):
                    rl = dmas(lambda s, c=c: s.dma_start(
                        out=rbuf[:, c % 2, :], in_=x2_d[c]),
                        waits=(([("v", vr3)] + s_x2spill) if c < 2
                               else [("v", vfin[c - 2])]))
                    E.v(lambda v, c=c: v.tensor_scalar(
                        obuf[:, c % 2, :], xres3[:, c, :],
                        st_mean[:, c:c + 1], st_rstd[:, c:c + 1],
                        ALU.subtract, ALU.mult),
                        waits=([("v", vr3)] if c == 0 else []) +
                              ([("v", vfin[c - 2]), qo_c[c - 2]]
                               if c >= 2 else []))
                    vfin[c] = E.v(lambda v, c=c: v.tensor_tensor(
                        obuf[:, c % 2, :], obuf[:, c % 2, :],
                        rbuf[:, c % 2, :], ALU.add), waits=[rl])
                    qo_c[c] = dmao(lambda s, c=c: s.dma_start(
                        out=y_d[c], in_=obuf[:, c % 2, :]),
                        waits=[("v", vfin[c]), ("s", E.c["s"])])
                E.op("s", None, waits=E.qfull("qO"))

        block = es.enter_context(nc.Block())

        def replay(lst):
            def run(eng):
                for waits, fn, inc, n in lst:
                    for snm, val in waits:
                        eng.wait_ge(sems[snm], val)
                    if fn is None:
                        continue
                    ins = fn(eng)
                    if inc is not None:
                        ins.then_inc(sems[inc], n)
            return run

        block.tensor(replay(E.q["t"]))
        block.vector(replay(E.q["v"]))
        block.scalar(replay(E.q["s"]))
        block.gpsimd(replay(E.q["g"]))
        block.sync(replay(E.q["y"]))

    return nc


def _prep_inputs(inputs):
    X = np.asarray(inputs["L0_attn_X"], np.float32)
    gam = _gammas()

    base = {}
    for i in range(NL):
        for nm, key in [("wq", "L0_attn_WQ"), ("wk", "L0_attn_WK"), ("wv", "L0_attn_WV")]:
            w = np.asarray(inputs[key][i], np.float32)
            base[f"{nm}{i}"] = np.ascontiguousarray(
                w.reshape(H, DC, 128, HD).transpose(0, 2, 1, 3))
        for nm, key in [("wg", "L0_attn_WG"), ("wo", "L0_attn_WO")]:
            w = np.asarray(inputs[key][i], np.float32)
            base[f"{nm}{i}"] = np.ascontiguousarray(
                w.reshape(DC, 128, D).transpose(1, 0, 2))
        w1 = np.asarray(inputs["L0_attn_ffn_w1"][i], np.float32)
        base[f"w1_{i}"] = np.ascontiguousarray(
            w1.reshape(DC, 128, FC, 128).transpose(2, 1, 0, 3))
        w2 = np.asarray(inputs["L0_attn_ffn_w2"][i], np.float32)
        base[f"w2_{i}"] = np.ascontiguousarray(
            w2.reshape(4, 8, 128, D).transpose(0, 2, 1, 3))
        for nm, key in [("ln1w", "L0_attn_ln1_w"), ("ln1b", "L0_attn_ln1_b"),
                        ("ln2w", "L0_attn_ln2_w"), ("ln2b", "L0_attn_ln2_b"),
                        ("gnw", "L0_attn_gn_w"), ("gnb", "L0_attn_gn_b")]:
            vec = np.asarray(inputs[key][i], np.float32)
            base[f"{nm}{i}"] = np.ascontiguousarray(
                np.broadcast_to(vec[None, :], (128, D)))
        b1 = np.asarray(inputs["L0_attn_ffn_b1"][i], np.float32)
        base[f"b1pp{i}"] = np.ascontiguousarray(b1.reshape(FC, 128).T)
        b2 = np.asarray(inputs["L0_attn_ffn_b2"][i], np.float32)
        base[f"b2bc{i}"] = np.ascontiguousarray(np.broadcast_to(b2[None, :], (128, D)))

    wi = np.asarray(inputs["L0_ffn_wi_W"], np.float32)
    wiT = np.ascontiguousarray(wi.T)
    base["wiT"] = np.ascontiguousarray(
        wiT.reshape(DC, 128, FC, 128).transpose(2, 1, 0, 3))
    wo = np.asarray(inputs["L0_ffn_wo_W"], np.float32)
    woT = np.ascontiguousarray(wo.T)
    base["woT"] = np.ascontiguousarray(
        woT.reshape(4, 8, 128, D).transpose(0, 2, 1, 3))
    gw = np.asarray(inputs["L0_ffn_gate_W"], np.float32)
    base["gwT"] = np.ascontiguousarray(gw.T.reshape(DC, 128, 8).transpose(1, 0, 2))
    la = np.asarray(inputs["L0_ffn_lora_A"], np.float32)
    base["acatT"] = np.ascontiguousarray(
        la.reshape(32, D).T.reshape(DC, 128, 32).transpose(1, 0, 2))
    lb = np.asarray(inputs["L0_ffn_lora_B"], np.float32)
    base["bcat"] = np.ascontiguousarray(lb.transpose(0, 2, 1).reshape(32, FF))
    base["gatebbc"] = np.ascontiguousarray(np.broadcast_to(
        np.asarray(inputs["L0_ffn_gate_b"], np.float32)[None, :], (128, 8)))
    base["wibpp"] = np.ascontiguousarray(
        np.asarray(inputs["L0_ffn_wi_b"], np.float32).reshape(FC, 128).T)
    base["wobbc"] = np.ascontiguousarray(np.broadcast_to(
        np.asarray(inputs["L0_ffn_wo_b"], np.float32)[None, :], (128, D)))

    sinq, cosq, sink, cosk = _xpos_tables()
    u = np.arange(128, dtype=np.float64)
    caus = (u[None, :] >= u[:, None]).astype(np.float32)
    gqv = np.stack([g ** (u + 1) for g in gam], 0)
    gkv = np.stack([g ** (-(u + 1)) for g in gam], 0)
    gq_t = np.ascontiguousarray(np.broadcast_to(gqv[None], (128, H, 128))).astype(np.float32)
    gk_t = np.ascontiguousarray(np.broadcast_to(gkv[None], (128, H, 128))).astype(np.float32)
    ident = np.eye(128, dtype=np.float32)
    rmt = np.zeros((128, 128), np.float32)
    for j in range(64):
        rmt[2 * j + 1, 2 * j] = -1.0
        rmt[2 * j, 2 * j + 1] = 1.0
    cpool = np.zeros((128, 12), np.float32)
    cpool[:, 0] = EPS
    for h in range(H):
        cpool[:, 1 + h] = np.float64(gam[h]) ** 128
    cpool[:, 9] = 1.0

    maps = []
    for core in range(N_CORES):
        b, q = core // 4, core % 4
        m = dict(base)
        sl = slice(q * T, (q + 1) * T)
        m["x0"] = np.ascontiguousarray(X[b, sl].reshape(NCH, 128, D))
        tb = np.stack([
            np.ascontiguousarray(t[sl].astype(np.float32).T.reshape(128, NCH, 128))
            for t in (sinq, cosq, sink, cosk)], 0)
        m["tabs"] = np.ascontiguousarray(tb)
        m["caus"] = caus
        m["gq"] = gq_t
        m["gk"] = gk_t
        m["identbf"] = ident.astype(BF)
        m["identf32"] = ident
        m["rmat"] = rmt.astype(BF)
        m["cpool"] = cpool
        mwa = np.zeros((H, 4), np.float64)
        for h, g in enumerate(gam):
            for cp in range(4):
                if cp < q:
                    mwa[h, cp] = np.float64(g) ** (2048 * (q - 1 - cp) + 128)
        m["mw"] = np.ascontiguousarray(
            np.broadcast_to(mwa.astype(np.float32)[None], (128, H, 4)))
        maps.append(m)
    return maps


_NC_CACHE = {}


def kernel(**inputs):
    maps = _prep_inputs(inputs)
    if STAGE not in _NC_CACHE:
        _NC_CACHE[STAGE] = build_program(STAGE)
    res = run_bass_kernel_spmd(_NC_CACHE[STAGE], maps, core_ids=list(range(N_CORES)))
    out = np.empty((2, S_FULL, D), np.float32)
    for core in range(N_CORES):
        b, q = core // 4, core % 4
        out[b, q * T:(q + 1) * T] = res.results[core]["y"].reshape(T, D)
    return out
